# version 26
# baseline (speedup 1.0000x reference)
import numpy as np
import concourse.bass as bass
import concourse.mybir as mybir

F32 = mybir.dt.float32
BF16 = mybir.dt.bfloat16
FP8 = mybir.dt.float8e4
AF = mybir.ActivationFunctionType
ALU = mybir.AluOpType
AX = mybir.AxisListType

ENGINES = ("tensor", "vector", "scalar", "gpsimd", "sync")


class OpRef:
    __slots__ = ("engine", "sem_idx", "target")
    def __init__(self, engine, sem_idx, target):
        self.engine = engine
        self.sem_idx = sem_idx
        self.target = target


class Builder:
    def __init__(self, nc):
        self.nc = nc
        self.prog = {e: [] for e in ENGINES}
        self.sem_names = []
        self.sem_counts = {}
        self.eng_sem = {}
        self.dma_sems = {}
        self.dma_rr = {}
        for e in ENGINES:
            self.eng_sem[e] = self._new_sem(f"c_{e}")
        for e, nr in (("sync", 12), ("gpsimd", 8), ("scalar", 4), ("vector", 4)):
            self.dma_sems[e] = [self._new_sem(f"d_{e}{i}") for i in range(nr)]
            self.dma_rr[e] = 0
        self.tiles = {}
        self.water = {}
        self.n_ops = 0

    def _new_sem(self, name):
        idx = len(self.sem_names)
        self.sem_names.append(name)
        self.sem_counts[idx] = 0
        return idx

    def _deps_for(self, reads, writes):
        deps = []
        for k in reads:
            t = self.tiles.get(k)
            if t and t["writer"] is not None:
                deps.append(t["writer"])
        for k in writes:
            t = self.tiles.get(k)
            if t:
                if t["writer"] is not None:
                    deps.append(t["writer"])
                deps.extend(t["readers"])
        return deps

    def emit(self, engine, fn, reads=(), writes=(), dma=False, dma_inc=None, extra_deps=()):
        deps = self._deps_for(reads, writes)
        deps.extend(extra_deps)
        bar = getattr(self, "_barrier", None)
        if bar is not None and bar.get(engine):
            deps.extend(d for d in bar[engine] if d.engine != engine or True)
            bar[engine] = []
        waits = []
        best = {}
        for d in deps:
            if d is None:
                continue
            if engine == "tensor" and d.engine == "tensor" and not dma:
                continue
            cur = best.get(d.sem_idx, 0)
            if d.target > cur:
                best[d.sem_idx] = d.target
        for semidx, val in best.items():
            wkey = (engine, semidx)
            if self.water.get(wkey, 0) >= val:
                continue
            self.water[wkey] = val
            waits.append((semidx, val))
        if dma:
            sems = self.dma_sems[engine]
            semidx = sems[self.dma_rr[engine] % len(sems)]
            self.dma_rr[engine] += 1
            inc = 16 if dma_inc is None else dma_inc
            prior = self.sem_counts[semidx]
            if prior > 0:
                wkey = (engine, semidx)
                if self.water.get(wkey, 0) < prior:
                    self.water[wkey] = prior
                    waits.append((semidx, prior))
        else:
            semidx = self.eng_sem[engine]
            inc = 1
        self.sem_counts[semidx] += inc
        target = self.sem_counts[semidx]
        ref = OpRef(engine, semidx, target)
        self.prog[engine].append((fn, waits, [(semidx, inc)]))
        for k in reads:
            self.tiles.setdefault(k, {"writer": None, "readers": []})["readers"].append(ref)
        for k in writes:
            t = self.tiles.setdefault(k, {"writer": None, "readers": []})
            t["writer"] = ref
            t["readers"] = []
        self.n_ops += 1
        return ref

    def global_barrier(self):
        refs = []
        for e in ENGINES:
            semidx = self.eng_sem[e]
            if self.sem_counts[semidx] > 0:
                refs.append(OpRef(e, semidx, self.sem_counts[semidx]))
            for semidx in self.dma_sems.get(e, ()):
                if self.sem_counts[semidx] > 0:
                    refs.append(OpRef(e, semidx, self.sem_counts[semidx]))
        self._barrier = {e: list(refs) for e in ENGINES}

    def barrier_ref(self, keys):
        out = []
        for k in keys:
            t = self.tiles.get(k)
            if t:
                if t["writer"]:
                    out.append(t["writer"])
                out.extend(t["readers"])
        return out

    def finalize(self, final_waits=()):
        nc = self.nc
        sems = []
        import contextlib
        stack = contextlib.ExitStack()
        for name in self.sem_names:
            sems.append(stack.enter_context(nc.semaphore(name)))
        blk = stack.enter_context(nc.Block())
        def runner(eng_name):
            def run(eng):
                for fn, waits, incs in self.prog[eng_name]:
                    for semidx, val in waits:
                        eng.wait_ge(sems[semidx], val)
                    ins = fn(eng)
                    for semidx, inc in incs:
                        ins.then_inc(sems[semidx], inc)
                if eng_name == "sync":
                    best = {}
                    for d in final_waits:
                        if d.target > best.get(d.sem_idx, 0):
                            best[d.sem_idx] = d.target
                    for semidx, val in best.items():
                        eng.wait_ge(sems[semidx], val)
            return run
        blk.tensor(runner("tensor"))
        blk.vector(runner("vector"))
        blk.scalar(runner("scalar"))
        blk.gpsimd(runner("gpsimd"))
        blk.sync(runner("sync"))
        stack.close()


class PsumPool:
    def __init__(self, nc, stack, n_f32=5, n_bf16=3):
        self.banks = []
        for i in range(n_f32):
            t = stack.enter_context(nc.psum_tensor(f"psf{i}", [128, 512], F32))
            self.banks.append(("f32", t, f"psum_f{i}"))
        for i in range(n_bf16):
            t = stack.enter_context(nc.psum_tensor(f"psb{i}", [128, 1024], BF16))
            self.banks.append(("bf16", t, f"psum_b{i}"))
        self.order = {"f32": [i for i, b in enumerate(self.banks) if b[0] == "f32"],
                      "bf16": [i for i, b in enumerate(self.banks) if b[0] == "bf16"]}
        self.rr = {"f32": 0, "bf16": 0}

    def acquire(self, dtype="f32"):
        lst = self.order[dtype]
        i = lst[self.rr[dtype] % len(lst)]
        self.rr[dtype] += 1
        kind, t, key = self.banks[i]
        return t, key


"""Deltanet megakernel program builder (per-core SPMD program).
Sequences sequential. fm = [feat x tok], tm = [tok x feat]. Chunks 64; c2 = 128-tok pair.
"""
import numpy as np
import contextlib
import concourse.bass as bass
import concourse.mybir as mybir

H = 8; DK = 128; DV = 128; CH = 64; D = 1024
DKS = float(DK) ** -0.5
V, A_, P_, G_, SY = "vector", "scalar", "tensor", "gpsimd", "sync"


def build(cfg):
    n_seq = cfg["n_seq"]; T = cfg["T"]; BLK = cfg["BLK"]
    n_blk = T // BLK
    nb_ch = BLK // CH
    nb_c2 = BLK // 128
    upto = cfg.get("upto", "full")
    phase = cfg.get("phase", "full")
    dbg = dict(cfg.get("debug", ()))
    do_coll = cfg.get("collective", True)
    n_cores = cfg.get("n_cores", 8)

    nc = bass.Bass("TRN2", target_bir_lowering=False)
    dt = nc.dram_tensor
    x_d = dt("x", [n_seq * T, D], F32, kind="ExternalInput").ap()
    wq_d = dt("wqkvz", [D, 4096], F32, kind="ExternalInput").ap()
    wba_d = dt("wba", [D, 16], F32, kind="ExternalInput").ap()
    cw_d = dt("convw", [3072, 4], F32, kind="ExternalInput").ap()
    dtb_d = dt("dtb", [H], F32, kind="ExternalInput").ap()
    alog_d = dt("alog", [H], F32, kind="ExternalInput").ap()
    nw_d = dt("nw", [DV], F32, kind="ExternalInput").ap()
    wo_d = dt("wout", [D, D], F32, kind="ExternalInput").ap()
    fw_d = dt("ffnw", [D, D], F32, kind="ExternalInput").ap()
    fb_d = dt("ffnb", [D], F32, kind="ExternalInput").ap()
    id_d = dt("ident", [128, 128], F32, kind="ExternalInput").ap()
    mall_d = dt("mask_all", [64, 1536], F32, kind="ExternalInput").ap()
    eh_d = dt("eh_sel", [128, 64], F32, kind="ExternalInput").ap()
    lrp_d = dt("lr_pat", [16, 512], F32, kind="ExternalInput").ap()
    rrp_d = dt("rr_pat", [4, 1536], F32, kind="ExternalInput").ap()
    e8_d = dt("e8_sel", [H, 8 * H], F32, kind="ExternalInput").ap()
    y_d = dt("y", [n_seq * T, D], F32, kind="ExternalOutput").ap()
    if phase == "attn":
        y1sp_d = dt("y1out", [n_seq * T, D], F32, kind="ExternalOutput").ap()
        amax_d = dt("amaxout", [1, 8], F32, kind="ExternalOutput").ap()
    else:
        y1sp_d = dt("y1spill", [n_seq * T, D], F32, kind="Internal").ap()
    if phase == "ffn":
        y1in_d = dt("y1in", [n_seq * T, D], F32, kind="ExternalInput").ap()
        amax8_d = dt("amax8", [1, 8], F32, kind="ExternalInput").ap()
    cci_d = dt("cc_in", [1, 8], F32, kind="Internal").ap()
    cco_d = dt("cc_out", [1, 8], F32, kind="Internal").ap()
    dbg_d = {k: dt("dbg_" + k, list(v), F32, kind="ExternalOutput").ap() for k, v in dbg.items()}

    B = Builder(nc)
    st = contextlib.ExitStack()
    attn_stack = contextlib.ExitStack()
    _scope = [attn_stack]
    sb = lambda name, shape, dtype: _scope[0].enter_context(nc.sbuf_tensor(name, shape, dtype))
    pool = PsumPool(nc, st, n_f32=5, n_bf16=3)

    _scope[0] = st
    identf = sb("identf", [128, 128], F32)
    one_col = sb("onecol", [128, 1], F32)
    eps5 = sb("eps5", [128, 1], F32)
    eps6 = sb("eps6", [128, 1], F32)
    ones1r = sb("ones1r", [1, 128], F32)
    stage = sb("stage", [128, 1024], F32)
    stage2 = sb("stage2", [128, 16], F32)
    y1_t = sb("y1", [128, D], F32)
    amax_col = sb("amaxcol", [128, 1], F32)
    gsc = sb("gsc", [1, 16], F32)
    sx_col = sb("sxcol", [128, 1], F32)
    fs_col = sb("fscol", [128, 1], F32)
    sw_col = sb("swcol", [128, 1], F32)
    _scope[0] = attn_stack
    wq = [sb(f"wq{c}", [128, 4096], BF16) for c in range(8)]
    wba = [sb(f"wba{c}", [128, 16], BF16) for c in range(8)]
    wo = [sb(f"wo{c}", [128, D], BF16) for c in range(8)]
    nw_col = sb("nwcol", [128, 1], F32)
    identb = sb("identb", [128, 128], BF16)
    mall = sb("mall", [64, nb_ch * 192], BF16)
    ehs = sb("ehs", [128, 64], BF16)
    e8s = sb("e8s", [H, nb_ch * H], BF16)
    ones8 = sb("ones8", [H, 128], F32)
    dt_col = sb("dtcol", [H, 1], F32)
    alog_col = sb("alogcol", [H, 1], F32)
    nega_col = sb("negacol", [H, 1], F32)
    cw_sb = sb("cw", [128, 96], F32)
    dg = [sb(f"dg{k}", [128, 128], BF16) for k in range(8)]
    S_t = sb("S", [128, H * DV], BF16)
    xf = [sb(f"xf{i}", [128, D], F32) for i in range(nb_c2)]
    qkv = [sb(f"qkv{i}", [128, BLK + 4], BF16) for i in range(24)]
    hist = [sb(f"hist{i}", [128, 4], BF16) for i in range(24)]
    xfm = [sb(f"xfm{c}", [128, BLK], BF16) for c in range(8)]
    z_tm = [[sb(f"z{i}_{p}", [64, D], BF16) for p in range(2)] for i in range(nb_c2)]
    AR_NAMES = ["ba_b", "ba_a", "beta", "g", "cum", "eg", "egl", "sqg", "skgN", "skl",
                "lnbrk", "lnrk", "m1", "m2", "m3", "rk", "rq", "alpha", "scr"]
    arena = {nm: sb(f"arn_{nm}", [H, BLK], F32) for nm in AR_NAMES}
    def ar(nm, rows=H):
        return arena[nm][0:rows, :]
    def ark(nm):
        return f"ar_{nm}"
    sc2 = [sb(f"sc2_{i}", [128, BLK], BF16) for i in range(2)]
    NRS = 2
    Lrow = [sb(f"Lr{i}", [4, BLK], F32) for i in range(NRS)]
    Rr = [sb(f"Rr{i}", [4, nb_ch * 192], F32) for i in range(NRS)]
    Dall = [sb(f"Dall{i}", [64, nb_ch * 192], BF16) for i in range(NRS)]
    NASET = 4
    NFSET = 4
    Ab = [sb(f"Ab{i}", [128, 256], BF16) for i in range(NASET)]
    ATb = [sb(f"ATb{i}", [128, 256], BF16) for i in range(NASET)]
    attnT = [[sb(f"aT{h}_{k}", [64, 128], BF16) for k in range(nb_c2)] for h in range(H)]
    NSLOT = 6
    fac = [[sb(f"fs{s}_{i}", [128, 256], BF16) for s in range(NSLOT)]
           for i in range(NFSET)]
    identf2 = sb("identf2", [128, 256], BF16)
    TmT = [[[sb(f"TmT{h}_{k}_{p}", [64, 64], BF16) for p in range(2)] for k in range(nb_c2)]
           for h in range(H)]
    kp_t = [sb(f"kp{p}", [64, H * DK], BF16) for p in range(2)]
    vb_t = [sb(f"vb{p}", [64, H * DV], BF16) for p in range(2)]
    scT = sb("scT", [128, 40], F32)
    aB = sb("aB", [128, nb_ch * H], F32)
    R_t = sb("R", [64, H * DV], BF16)
    Rs_t = stage[0:64, 0:H * DV]
    vn_t = sb("vn", [64, H * DV], BF16)
    h_t = sb("ht", [64, H * DV], F32)
    rst_t = sb("rst", [64, H], F32)
    hg_t = [sb("hg0", [128, H * DV], BF16)] * 2
    hgf = sb("hgf", [128, H * DV], BF16)
    Sscr = sb("Sscr", [128, H * DV], BF16)
    ffn_tiles = {}

    def dma(queue, out, in_, wkeys, rkeys=()):
        return B.emit(queue, lambda e, o=out, i=in_: e.dma_start(out=o, in_=i),
                      reads=rkeys, writes=wkeys, dma=True)

    out_refs = []

    def dbg_store(name, src_ap, rkeys, row0=0):
        if name not in dbg_d:
            return
        shp = dbg[name]
        r = B.emit(G_, lambda e: e.dma_start(out=dbg_d[name][row0:row0 + src_ap.shape[0], :],
                                             in_=src_ap),
                   reads=rkeys, writes=(), dma=True)
        out_refs.append(r)

    def phase_w():
        for c in range(8):
            dma(G_, wq[c][:], wq_d[c * 128:(c + 1) * 128, :], (f"wq{c}",))
            dma(G_, wba[c][:], wba_d[c * 128:(c + 1) * 128, :], (f"wba{c}",))
        dma(SY, nw_col[:], nw_d.rearrange("(p one) -> p one", one=1), ("nw",))
        dma(SY, identf[:], id_d, ("identf",))
        dma(G_, identf2[:, 0:128], id_d, ("identf2",))
        dma(G_, identf2[:, 128:256], id_d, ("identf2",))
        dma(G_, identb[:], id_d, ("identb",))
        dma(G_, mall[:], mall_d[:, 0:nb_ch * 192], ("mall",))
        dma(G_, ehs[:], eh_d, ("ehs",))
        dma(G_, e8s[:], e8_d[:, 0:nb_ch * H], ("e8s",))
        dma(SY, dt_col[:], dtb_d.rearrange("(p one) -> p one", one=1), ("dtcol",))
        dma(SY, alog_col[:], alog_d.rearrange("(p one) -> p one", one=1), ("alogcol",))
        dma(SY, cw_sb[:].rearrange("p (a b) -> p a b", a=24),
            cw_d.rearrange("(a p) b -> p a b", p=128), ("cw",))

        for c in range(8):
            dma(SY, stage[:], wo_d[c * 128:(c + 1) * 128, :], ("stage", "Rs0", "Rs1"))
            B.emit(V, lambda e, c=c: e.tensor_scalar(wo[c][:], stage[:], nw_col[:, 0:1],
                                                     None, ALU.mult),
                   reads=("stage", "nw"), writes=(f"wo{c}",))
        B.emit(V, lambda e: e.memset(amax_col[:], 0.0), writes=("amax",))
        B.emit(A_, lambda e: e.activation(nega_col[:], alog_col[:], AF.Exp),
               reads=("alogcol",), writes=("negacol",))
        B.emit(V, lambda e: e.tensor_scalar(nega_col[:], nega_col[:], -1.0, None, ALU.mult),
               reads=("negacol",), writes=("negacol",))
        B.emit(V, lambda e: e.memset(one_col[:], 1.0), writes=("onecol",))
        B.emit(V, lambda e: e.memset(eps6[:], 1e-6), writes=("eps6",))
        B.emit(V, lambda e: e.memset(eps5[:], 1e-5), writes=("eps5",))
        B.emit(V, lambda e: e.memset(ones1r[:], 1.0), writes=("ones1r",))
        B.emit(V, lambda e: e.memset(ones8[:], 1.0), writes=("ones8",))
        for i in range(NASET):
            B.emit(V, lambda e, i=i: e.memset(Ab[i][:], 0.0), writes=(f"Ab{i}",))
            B.emit(V, lambda e, i=i: e.memset(ATb[i][:], 0.0), writes=(f"ATb{i}",))
        for i in range(NRS):
            dma(SY, Lrow[i][:], lrp_d[0:4, 0:BLK], (f"Lr{i}",))
            dma(SY, Rr[i][:], rrp_d[0:4, 0:nb_ch * 192], (f"Rr{i}",))



    def qsl(h, ci):
        return qkv[0 * 8 + h][:, 3 + ci * CH: 3 + (ci + 1) * CH]
    def ksl(h, ci):
        return qkv[1 * 8 + h][:, 3 + ci * CH: 3 + (ci + 1) * CH]
    def qsl2(h, c2):
        return qkv[0 * 8 + h][:, 3 + c2 * 128: 3 + (c2 + 1) * 128]
    def ksl2(h, c2):
        return qkv[1 * 8 + h][:, 3 + c2 * 128: 3 + (c2 + 1) * 128]
    def vsl2(h, c2):
        return qkv[2 * 8 + h][:, 3 + c2 * 128: 3 + (c2 + 1) * 128]

    def seq_init(s):
        B.emit(V, lambda e: e.memset(S_t[:], 0.0), writes=("S0", "S1"))
        for i in range(24):
            B.emit(V, lambda e, i=i: e.memset(qkv[i][:, 0:3], 0.0), writes=(f"qkv{i}",))

    def load_transpose_x(s, b):
        for i in range(nb_c2):
            dma(SY, xf[i][:], x_d[s * T + b * BLK + i * 128: s * T + b * BLK + (i + 1) * 128, :],
                (f"xf{i}",))
        for c in range(8):
            bank, bk = pool.acquire("f32")
            for i in range(nb_c2):
                B.emit(P_, lambda e, c=c, i=i, bank=bank: e.transpose(
                    bank[:, i * 128:(i + 1) * 128], xf[i][:, c * 128:(c + 1) * 128], identf[:]),
                    reads=(f"xf{i}", "identf"), writes=(bk,))
            B.emit(A_, lambda e, c=c, bank=bank: e.activation(xfm[c][:, 0:BLK],
                                                              bank[:, 0:nb_c2 * 128], AF.Copy),
                   reads=(bk,), writes=(f"xfm{c}",))

    def projections(s, b):
        for ot in range(24):
            h, part = ot // 3, ot % 3
            ti = part * 8 + h
            bank, bk = pool.acquire("f32")
            for c in range(8):
                B.emit(P_, lambda e, c=c, h=h, part=part, bank=bank: e.matmul(
                    bank[:, 0:BLK], wq[c][:, h * 384 + part * 128: h * 384 + (part + 1) * 128],
                    xfm[c][:, 0:BLK], start=(c == 0), stop=(c == 7)),
                    reads=(f"wq{c}", f"xfm{c}"), writes=(bk,))
            if b > 0:
                B.emit(V, lambda e, ti=ti: e.tensor_copy(qkv[ti][:, 0:3], hist[ti][:, 0:3]),
                       reads=(f"hist{ti}",), writes=(f"qkv{ti}",))
            B.emit(A_, lambda e, ti=ti, bank=bank: e.activation(qkv[ti][:, 3:3 + BLK],
                                                                bank[:, 0:BLK], AF.Copy),
                   reads=(bk,), writes=(f"qkv{ti}",))
            B.emit(V, lambda e, ti=ti: e.tensor_copy(hist[ti][:, 0:3], qkv[ti][:, BLK:BLK + 3]),
                   reads=(f"qkv{ti}",), writes=(f"hist{ti}",))
        for i in range(nb_c2):
            for zb in range(2):
                bank, bk = pool.acquire("f32")
                for c in range(8):
                    B.emit(P_, lambda e, c=c, i=i, zb=zb, bank=bank: e.matmul(
                        bank[:, 0:512], xfm[c][:, i * 128:(i + 1) * 128],
                        wq[c][:, 3072 + zb * 512: 3072 + (zb + 1) * 512],
                        start=(c == 0), stop=(c == 7)),
                        reads=(f"wq{c}", f"xfm{c}"), writes=(bk,))
                for p in range(2):
                    B.emit(A_, lambda e, i=i, zb=zb, p=p, bank=bank: e.activation(
                        z_tm[i][p][:, zb * 512:(zb + 1) * 512],
                        bank[p * 64:(p + 1) * 64, 0:512], AF.Silu),
                        reads=(bk,), writes=(f"z{i}_{p}",))
        for part, nm in ((0, "ba_b"), (1, "ba_a")):
            bank, bk = pool.acquire("f32")
            for c in range(8):
                B.emit(P_, lambda e, c=c, part=part, bank=bank: e.matmul(
                    bank[0:H, 0:BLK], wba[c][:, part * 8:(part + 1) * 8],
                    xfm[c][:, 0:BLK], start=(c == 0), stop=(c == 7)),
                    reads=(f"wba{c}", f"xfm{c}"), writes=(bk,))
            B.emit(V, lambda e, nm=nm, bank=bank: e.tensor_copy(ar(nm), bank[0:H, 0:BLK]),
                   reads=(bk,), writes=(ark(nm),))

    def conv(s, b):
        for ot in range(24):
            h, part = ot // 3, ot % 3
            ti = part * 8 + h
            for k in range(4):
                dgi = (ot % 2) * 4 + k
                B.emit(V, lambda e, k=k, ti=ti, dgi=dgi: e.tensor_scalar(
                    dg[dgi][:], identb[:], cw_sb[:, ti * 4 + k: ti * 4 + k + 1], None, ALU.mult),
                    reads=("identb", "cw"), writes=(f"dg{dgi}",))
            bank, bk = pool.acquire("f32")
            for k in range(4):
                dgi = (ot % 2) * 4 + k
                B.emit(P_, lambda e, k=k, ti=ti, dgi=dgi, bank=bank: e.matmul(
                    bank[:, 0:BLK], dg[dgi][:], qkv[ti][:, k: k + BLK],
                    start=(k == 0), stop=(k == 3)),
                    reads=(f"dg{dgi}", f"qkv{ti}"), writes=(bk,))
            B.emit(A_, lambda e, ti=ti, bank=bank: e.activation(
                qkv[ti][:, 3:3 + BLK], bank[:, 0:BLK], AF.Silu),
                reads=(bk,), writes=(f"qkv{ti}",))

    def gchain(s, b):
        B.emit(A_, lambda e: e.activation(ar("beta"), ar("ba_b"), AF.Exp, scale=-1.0),
               reads=(ark("ba_b"),), writes=(ark("beta"),))
        B.emit(V, lambda e: e.tensor_scalar(ar("beta"), ar("beta"), 1.0, None, ALU.add),
               reads=(ark("beta"),), writes=(ark("beta"),))
        B.emit(V, lambda e: e.reciprocal(ar("beta"), ar("beta")),
               reads=(ark("beta"),), writes=(ark("beta"),))
        B.emit(V, lambda e: e.tensor_scalar(ar("scr"), ar("ba_a"), dt_col[:, 0:1], 0.0,
                                            ALU.add, ALU.max),
               reads=(ark("ba_a"), "dtcol"), writes=(ark("scr"),))
        B.emit(A_, lambda e: e.activation(ar("g"), ar("ba_a"), AF.Abs, bias=dt_col[:, 0:1]),
               reads=(ark("ba_a"), "dtcol"), writes=(ark("g"),))
        B.emit(A_, lambda e: e.activation(ar("g"), ar("g"), AF.Exp, scale=-1.0),
               reads=(ark("g"),), writes=(ark("g"),))
        B.emit(A_, lambda e: e.activation(ar("g"), ar("g"), AF.Ln, bias=one_col[0:H, 0:1]),
               reads=(ark("g"), "onecol"), writes=(ark("g"),))
        B.emit(V, lambda e: e.tensor_add(ar("g"), ar("g"), ar("scr")),
               reads=(ark("g"), ark("scr")), writes=(ark("g"),))
        B.emit(V, lambda e: e.tensor_scalar(ar("g"), ar("g"), nega_col[:, 0:1], None, ALU.mult),
               reads=(ark("g"), "negacol"), writes=(ark("g"),))
        src, dst = "g", "scr"
        for stp in (1, 2, 4, 8, 16, 32):
            def shift_add(e, stp=stp, src=src, dst=dst):
                a3 = ar(src).rearrange("p (n c) -> p n c", c=CH)
                d3 = ar(dst).rearrange("p (n c) -> p n c", c=CH)
                return e.tensor_add(d3[:, :, stp:], a3[:, :, stp:], a3[:, :, 0:CH - stp])
            def head_copy(e, stp=stp, src=src, dst=dst):
                a3 = ar(src).rearrange("p (n c) -> p n c", c=CH)
                d3 = ar(dst).rearrange("p (n c) -> p n c", c=CH)
                return e.tensor_copy(d3[:, :, 0:stp], a3[:, :, 0:stp])
            B.emit(V, shift_add, reads=(ark(src),), writes=(ark(dst),))
            B.emit(V, head_copy, reads=(ark(src),), writes=(ark(dst),))
            src, dst = dst, src
        if src != "cum":
            B.emit(V, lambda e, src=src: e.tensor_copy(ar("cum"), ar(src)),
                   reads=(ark(src),), writes=(ark("cum"),))
        B.emit(A_, lambda e: e.activation(ar("eg"), ar("cum"), AF.Exp),
               reads=(ark("cum"),), writes=(ark("eg"),))
        def egl_sub(e):
            c3 = ar("cum").rearrange("p (n c) -> p n c", c=CH)
            gl = c3[:, :, CH - 1:CH].broadcast_to([H, nb_ch, CH])
            return e.tensor_sub(ar("egl").rearrange("p (n c) -> p n c", c=CH), gl, c3)
        B.emit(V, egl_sub, reads=(ark("cum"),), writes=(ark("egl"),))
        B.emit(A_, lambda e: e.activation(ar("egl"), ar("egl"), AF.Exp),
               reads=(ark("egl"),), writes=(ark("egl"),))

    def l2norm(s, b):
        for part, nm in ((0, "rq"), (1, "rk")):
            bank, bk = pool.acquire("f32")
            for h in range(H):
                ti = part * 8 + h
                sct, sck = sc2[h % 2], f"sc2_{h % 2}"
                B.emit(V, lambda e, ti=ti, sct=sct: e.tensor_mul(
                    sct[:, 0:BLK], qkv[ti][:, 3:3 + BLK], qkv[ti][:, 3:3 + BLK]),
                       reads=(f"qkv{ti}",), writes=(sck,))
                B.emit(P_, lambda e, h=h, bank=bank, sct=sct: e.matmul(
                    bank[0:H, 0:BLK], ehs[:, h * 8:(h + 1) * 8], sct[:, 0:BLK],
                    start=(h == 0), stop=(h == 7)),
                    reads=("ehs", sck), writes=(bk,))
            B.emit(A_, lambda e, bank=bank: e.activation(ar("scr"), bank[0:H, 0:BLK], AF.Ln,
                                                         bias=eps6[0:H, 0:1]),
                   reads=(bk, "eps6"), writes=(ark("scr"),))
            B.emit(A_, lambda e, nm=nm: e.activation(ar(nm), ar("scr"), AF.Exp, scale=-0.5),
                   reads=(ark("scr"),), writes=(ark(nm),))

    def scalerows(s, b, rset):
        def em(eng, fn, r, w):
            B.emit(eng, fn, reads=tuple(ark(k) if k in arena else k for k in r), writes=(ark(w),))
        em(V, lambda e: e.scalar_tensor_tensor(ar("sqg"), ar("rq"), DKS, ar("eg"),
                                               ALU.mult, ALU.mult), ("rq", "eg"), "sqg")
        em(V, lambda e: e.scalar_tensor_tensor(ar("scr"), ar("rk"), -1.0, ar("beta"),
                                               ALU.mult, ALU.mult), ("rk", "beta"), "scr")
        em(V, lambda e: e.tensor_mul(ar("skgN"), ar("scr"), ar("eg")), ("scr", "eg"), "skgN")
        em(V, lambda e: e.tensor_mul(ar("skl"), ar("rk"), ar("egl")), ("rk", "egl"), "skl")
        em(V, lambda e: e.tensor_mul(ar("scr"), ar("beta"), ar("rk")), ("beta", "rk"), "scr")
        em(A_, lambda e: e.activation(ar("lnbrk"), ar("scr"), AF.Ln), ("scr",), "lnbrk")
        em(A_, lambda e: e.activation(ar("lnrk"), ar("rk"), AF.Ln), ("rk",), "lnrk")
        em(A_, lambda e: e.activation(ar("m3"), ar("rq"), AF.Ln, scale=DKS), ("rq",), "m3")
        em(V, lambda e: e.tensor_sub(ar("m1"), ar("lnrk"), ar("cum")), ("lnrk", "cum"), "m1")
        em(V, lambda e: e.tensor_add(ar("m2"), ar("cum"), ar("lnbrk")), ("cum", "lnbrk"), "m2")
        em(V, lambda e: e.tensor_add(ar("m3"), ar("m3"), ar("cum")), ("m3", "cum"), "m3")
        def acopy(e):
            e3 = ar("eg").rearrange("p (n c) -> p n c", c=CH)
            dst = ar("alpha")[:, 0:nb_ch].unsqueeze(2)
            return e.tensor_copy(dst, e3[:, :, CH - 1:CH])
        B.emit(V, acopy, reads=(ark("eg"),), writes=(ark("alpha"),))
        def e8mul(e):
            al = ar("alpha")[:, 0:nb_ch]
            alb = al.unsqueeze(2).broadcast_to([H, nb_ch, H])
            e8v = e8s[:].rearrange("p (n h) -> p n h", h=H)
            return e.tensor_mul(ar("scr")[:, 0:nb_ch * H].rearrange("p (n h) -> p n h", h=H),
                                e8v, alb)
        B.emit(V, e8mul, reads=(ark("alpha"), "e8s"), writes=(ark("scr"),))
        bank, bk = pool.acquire("f32")
        B.emit(P_, lambda e, bank=bank: e.matmul(bank[:, 0:nb_ch * H], ones8[:],
                                                 ar("scr")[:, 0:nb_ch * H], start=True, stop=True),
               reads=(ark("scr"), "ones8"), writes=(bk,))
        B.emit(V, lambda e, bank=bank: e.tensor_copy(aB[:], bank[:, 0:nb_ch * H]),
               reads=(bk,), writes=("aB",))
        return

    def prescan_head(s, b, h, dset):
        rset = dset
        for (row, srcnm) in ((0, "cum"), (2, "lnrk"), (3, "lnbrk")):
            B.emit(SY, lambda e, row=row, srcnm=srcnm: e.dma_start(
                out=Lrow[rset][row:row + 1, :], in_=ar(srcnm)[h:h + 1, :]),
                reads=(ark(srcnm),), writes=(f"Lr{rset}",), dma=True)
        for (off, srcnm) in ((0, "m1"), (64, "m2"), (128, "m3")):
            def rdma(e, off=off, srcnm=srcnm):
                dst = Rr[rset][1:2, :].rearrange("p (n q) -> p n q", q=192)[:, :, off:off + 64]
                return e.dma_start(out=dst, in_=ar(srcnm)[h:h + 1, :].rearrange(
                    "p (n q) -> p n q", q=64))
            B.emit(A_, rdma, reads=(ark(srcnm),), writes=(f"Rr{rset}",), dma=True)
        nbh = (nb_ch + 1) // 2
        banks = [pool.acquire("f32") for _ in range((nb_ch + 1) // 2)]
        for ci in range(nb_ch):
            sl = slice(ci * CH, (ci + 1) * CH)
            bank, bk = banks[ci // 2]
            B.emit(P_, lambda e, sl=sl, ci=ci, bank=bank: e.matmul(
                bank[0:64, (ci % 2) * 192:(ci % 2) * 192 + 192], Lrow[rset][:, sl],
                Rr[rset][:, ci * 192:(ci + 1) * 192], start=True, stop=True),
                reads=(f"Lr{rset}", f"Rr{rset}"), writes=(bk,))
        for bi, (bank, bk) in enumerate(banks):
            w2 = min(nb_ch - bi * 2, 2) * 192
            B.emit(V, lambda e, bank=bank, bi=bi, w2=w2: e.tensor_add(
                bank[0:64, 0:w2], bank[0:64, 0:w2], mall[:, bi * 384: bi * 384 + w2]),
                reads=(bk, "mall"), writes=(bk,))
            B.emit(A_, lambda e, bank=bank, bi=bi, w2=w2: e.activation(
                Dall[dset][:, bi * 384: bi * 384 + w2], bank[0:64, 0:w2], AF.Exp),
                reads=(bk,), writes=(f"Dall{dset}",))
        aset = h % NASET
        for c2 in range(nb_c2):
            prescan_grams(s, b, h, c2, dset, aset)

    def prescan_grams(s, b, h, c2, dset, aset):
        bank, bk = pool.acquire("f32")
        for par in range(2):
            ci = c2 * 2 + par
            B.emit(P_, lambda e, ci=ci, par=par, bank=bank: e.matmul(
                bank[0:64, par * 128: par * 128 + 64], ksl(h, ci), ksl(h, ci),
                start=True, stop=True), reads=(f"qkv{8 + h}",), writes=(bk,))
            B.emit(P_, lambda e, ci=ci, par=par, bank=bank: e.matmul(
                bank[0:64, par * 128 + 64: par * 128 + 128], ksl(h, ci), qsl(h, ci),
                start=True, stop=True), reads=(f"qkv{8 + h}", f"qkv{h}"), writes=(bk,))
        for par in range(2):
            ci = c2 * 2 + par
            pb = par * 64
            co = c2 * 128 + pb
            B.emit(V, lambda e, ci=ci, pb=pb, co=co, par=par, bank=bank: e.tensor_mul(
                Ab[aset][pb:pb + 64, co:co + 64], bank[0:64, par * 128: par * 128 + 64],
                Dall[dset][:, ci * 192:ci * 192 + 64]),
                reads=(bk, f"Dall{dset}"), writes=(f"Ab{aset}",))
            B.emit(V, lambda e, ci=ci, pb=pb, co=co, par=par, bank=bank: e.tensor_mul(
                ATb[aset][pb:pb + 64, co:co + 64], bank[0:64, par * 128: par * 128 + 64],
                Dall[dset][:, ci * 192 + 64:ci * 192 + 128]),
                reads=(bk, f"Dall{dset}"), writes=(f"ATb{aset}",))
        def at_mul(e):
            kq = bank[0:64, 0:256].rearrange("p (n q) -> p n q", q=128)[:, :, 64:128]
            dui = Dall[dset][:, :].rearrange("p (n q) -> p n q", q=192)[
                :, 2 * c2:2 * c2 + 2, 128:192]
            at = attnT[h][c2][:].rearrange("p (n q) -> p n q", q=64)
            return e.tensor_mul(at, kq, dui)
        B.emit(V, at_mul, reads=(bk, f"Dall{dset}"), writes=(f"aT{h}_{c2}",))

    def factor_chains(s, b, heads):
        SLOT = {"ImAT": 0, "A2": 1, "A2T": 2, "T1": 3, "A4": 0, "A4T": 4,
                "T2": 5, "A8": 1, "A8T": 2, "T3": 3, "A16": 0, "A16T": 4,
                "T4": 5, "A32": 1}
        ctx = []
        for (h, aset, fset) in heads:
            ctx.append((h, Ab[aset], f"Ab{aset}", ATb[aset], f"ATb{aset}",
                        fac[fset], fset))
        def tile(c, nm):
            (h, Abt, kA, ATt, kAT, FS, fset) = c
            if nm == "A": return Abt, kA
            if nm == "AT": return ATt, kAT
            s = SLOT[nm]
            return FS[s], f"fs{s}_{fset}"
        def sq_step(c, dstnm, ltnm, rhnm, evac_eng):
            dst, dkey = tile(c, dstnm)
            lt, lkey = tile(c, ltnm)
            rh, rkey = tile(c, rhnm)
            bank2, bk2 = pool.acquire("f32")
            for p in range(2):
                B.emit(P_, lambda e, bank2=bank2, lt=lt, rh=rh, p=p: e.matmul(
                    bank2[:, p * 128:(p + 1) * 128], lt[:, p * 128:(p + 1) * 128],
                    rh[:, p * 128:(p + 1) * 128], start=True, stop=True),
                    reads=(lkey, rkey), writes=(bk2,))
            if evac_eng == A_:
                B.emit(A_, lambda e, bank2=bank2, dst=dst: e.activation(
                    dst[:], bank2[:, 0:256], AF.Copy), reads=(bk2,), writes=(dkey,))
            else:
                B.emit(V, lambda e, bank2=bank2, dst=dst: e.tensor_copy(dst[:], bank2[:, 0:256]),
                       reads=(bk2,), writes=(dkey,))
        def chain_step(c, dstnm, pwnm, rhnm):
            dst, dkey = tile(c, dstnm)
            pw, pwkey = tile(c, pwnm)
            rh, rkey = tile(c, rhnm)
            bank2, bk2 = pool.acquire("f32")
            for p in range(2):
                B.emit(P_, lambda e, bank2=bank2, pw=pw, rh=rh, p=p: e.matmul(
                    bank2[:, p * 128:(p + 1) * 128], pw[:, p * 128:(p + 1) * 128],
                    rh[:, p * 128:(p + 1) * 128], start=(p == 0), stop=False),
                    reads=(pwkey, rkey), writes=(bk2,))
            B.emit(P_, lambda e, bank2=bank2, rh=rh: e.matmul(
                bank2[:, 0:256], identb[:], rh[:, 0:256], start=False, stop=True),
                reads=("identb", rkey), writes=(bk2,))
            B.emit(A_, lambda e, bank2=bank2, dst=dst: e.activation(
                dst[:], bank2[:, 0:256], AF.Copy), reads=(bk2,), writes=(dkey,))
        def sub_step(c):
            dst, dkey = tile(c, "ImAT")
            ATt, kAT = tile(c, "AT")
            B.emit(V, lambda e, dst=dst, ATt=ATt: e.tensor_sub(dst[:], identf2[:], ATt[:]),
                   reads=("identf2", kAT), writes=(dkey,))
        steps = [
            sub_step,
            lambda c, k=0: sq_step(c, "A2", "AT", "A", A_ if k % 2 else V),
            lambda c, k=1: sq_step(c, "A2T", "A", "AT", A_ if k % 2 else V),
            lambda c: chain_step(c, "T1", "A2", "ImAT"),
            lambda c, k=0: sq_step(c, "A4", "A2T", "A2", A_ if k % 2 else V),
            lambda c, k=1: sq_step(c, "A4T", "A2", "A2T", A_ if k % 2 else V),
            lambda c: chain_step(c, "T2", "A4", "T1"),
            lambda c, k=0: sq_step(c, "A8", "A4T", "A4", A_ if k % 2 else V),
            lambda c, k=1: sq_step(c, "A8T", "A4", "A4T", A_ if k % 2 else V),
            lambda c: chain_step(c, "T3", "A8", "T2"),
            lambda c, k=0: sq_step(c, "A16", "A8T", "A8", A_ if k % 2 else V),
            lambda c, k=1: sq_step(c, "A16T", "A8", "A8T", A_ if k % 2 else V),
            lambda c: chain_step(c, "T4", "A16", "T3"),
            lambda c, k=0: sq_step(c, "A32", "A16T", "A16", A_ if k % 2 else V),
        ]
        for stp in steps:
            for c in ctx:
                stp(c)
        for c in ctx:
            (h, Abt, kA, ATt, kAT, FS, fset) = c
            a32, k32 = tile(c, "A32")
            t4, k4 = tile(c, "T4")
            bank2, bk2 = pool.acquire("f32")
            for p in range(2):
                B.emit(P_, lambda e, bank2=bank2, a32=a32, t4=t4, p=p: e.matmul(
                    bank2[:, p * 128:(p + 1) * 128], a32[:, p * 128:(p + 1) * 128],
                    t4[:, p * 128:(p + 1) * 128], start=True, stop=True),
                    reads=(k32, k4), writes=(bk2,))
            for c2 in range(nb_c2):
                for par in range(2):
                    pb = par * 64
                    co = c2 * 128 + pb
                    B.emit(V, lambda e, bank2=bank2, t4=t4, h=h, c2=c2, par=par, pb=pb, co=co:
                           e.tensor_add(TmT[h][c2][par][:], bank2[pb:pb + 64, co:co + 64],
                                        t4[pb:pb + 64, co:co + 64]),
                           reads=(bk2, k4), writes=(f"TmT{h}_{c2}_{par}",))

    def c2_prep(s, b, c2):
        bank, bk = pool.acquire("f32")
        for gi, nm in enumerate(("sqg", "skgN", "skl", "beta", "cum")):
            B.emit(P_, lambda e, gi=gi, nm=nm, bank=bank: e.transpose(
                bank[:, gi * 8:(gi + 1) * 8], ar(nm)[:, c2 * 128:(c2 + 1) * 128],
                identf[0:8, 0:8]),
                reads=(ark(nm), "identf"), writes=(bk,))
        B.emit(V, lambda e, bank=bank: e.tensor_copy(scT[:], bank[:, 0:40]),
               reads=(bk,), writes=("scT",))
        for kind, tiles, scol in (("k", kp_t, 16), ("v", vb_t, 24)):
            bankb, bkb = pool.acquire("bf16")
            for h in range(H):
                src = ksl2(h, c2) if kind == "k" else vsl2(h, c2)
                ti = (1 if kind == "k" else 2) * 8 + h
                B.emit(P_, lambda e, h=h, src=src, bankb=bankb: e.transpose(
                    bankb[:, h * 128:(h + 1) * 128], src, identb[:]),
                    reads=(f"qkv{ti}", "identb"), writes=(bkb,))
            for par in range(2):
                def evac(e, par=par, bankb=bankb, tiles=tiles, scol=scol):
                    pv = bankb[par * 64:(par + 1) * 64, :].rearrange("p (h d) -> p h d", d=128)
                    scs = scT[par * 64:(par + 1) * 64, scol:scol + 8]
                    scb = scs.unsqueeze(2).broadcast_to([64, H, 128])
                    dst = tiles[par][:].rearrange("p (h d) -> p h d", d=128)
                    return e.tensor_mul(dst, pv, scb)
                B.emit(V, evac, reads=(bkb, "scT"), writes=(f"{'kp' if kind=='k' else 'vb'}{par}",))

    def scan_step(s, b, ci):
        par, c2 = ci % 2, ci // 2
        kP, kV = [], []
        bankP = []
        for half in range(2):
            bank, bk = pool.acquire("f32")
            bankP.append((bank, bk))
            for hh in range(4):
                h = half * 4 + hh
                B.emit(P_, lambda e, h=h, hh=hh, bank=bank: e.matmul(
                    bank[0:64, hh * 128:(hh + 1) * 128], ksl(h, ci),
                    S_t[:, h * 128:(h + 1) * 128], start=True, stop=True),
                    reads=(f"qkv{8 + h}", f"S{h // 4}"), writes=(bk,))
        for half, (bank, bk) in enumerate(bankP):
            def rmul(e, half=half, bank=bank):
                pv = bank[0:64, 0:512].rearrange("p (h d) -> p h d", d=128)
                scs = scT[par * 64:(par + 1) * 64, 8 + half * 4: 8 + half * 4 + 4]
                scb = scs.unsqueeze(2).broadcast_to([64, 4, 128])
                dst = Rs_t[:, half * 512:(half + 1) * 512].rearrange("p (h d) -> p h d", d=128)
                return e.tensor_mul(dst, pv, scb)
            B.emit(V, rmul, reads=(bk, "scT"), writes=(f"Rs{half}",))
        for half in range(2):
            B.emit(V, lambda e, half=half: e.tensor_add(
                R_t[:, half * 512:(half + 1) * 512], Rs_t[:, half * 512:(half + 1) * 512],
                vb_t[par][:, half * 512:(half + 1) * 512]),
                reads=(f"Rs{half}", f"vb{par}"), writes=(f"R{half}",))
        bankV = []
        for half in range(2):
            bank, bk = pool.acquire("f32")
            bankV.append((bank, bk))
            for hh in range(4):
                h = half * 4 + hh
                B.emit(P_, lambda e, h=h, hh=hh, bank=bank: e.matmul(
                    bank[0:64, hh * 128:(hh + 1) * 128], TmT[h][c2][par][:],
                    R_t[:, h * 128:(h + 1) * 128], start=True, stop=True),
                    reads=(f"TmT{h}_{c2}_{par}", f"R{h // 4}"), writes=(bk,))
            B.emit(A_, lambda e, half=half, bank=bank: e.activation(
                vn_t[:, half * 512:(half + 1) * 512], bank[0:64, 0:512], AF.Copy),
                reads=(bk,), writes=(f"vn{half}",))
        bankO = []
        for half in range(2):
            bank, bk = pool.acquire("f32")
            bankO.append((bank, bk))
            for hh in range(4):
                h = half * 4 + hh
                B.emit(P_, lambda e, h=h, hh=hh, bank=bank: e.matmul(
                    bank[0:64, hh * 128:(hh + 1) * 128], qsl(h, ci),
                    S_t[:, h * 128:(h + 1) * 128], start=True, stop=True),
                    reads=(f"qkv{h}", f"S{h // 4}"), writes=(bk,))
            def omul(e, half=half, bank=bank):
                pv = bank[0:64, 0:512].rearrange("p (h d) -> p h d", d=128)
                scs = scT[par * 64:(par + 1) * 64, half * 4: half * 4 + 4]
                scb = scs.unsqueeze(2).broadcast_to([64, 4, 128])
                dst = h_t[:, half * 512:(half + 1) * 512].rearrange("p (h d) -> p h d", d=128)
                return e.tensor_mul(dst, pv, scb)
            B.emit(V, omul, reads=(bk, "scT"), writes=("ht",))
        for half in range(2):
            bank, bk = pool.acquire("f32")
            for hh in range(4):
                h = half * 4 + hh
                B.emit(P_, lambda e, h=h, hh=hh, bank=bank: e.matmul(
                    bank[0:64, hh * 128:(hh + 1) * 128], attnT[h][c2][:, par * 64:(par + 1) * 64],
                    vn_t[:, h * 128:(h + 1) * 128], start=True, stop=True),
                    reads=(f"aT{h}_{c2}", f"vn{h // 4}"), writes=(bk,))
            B.emit(V, lambda e, half=half, bank=bank: e.tensor_add(
                h_t[:, half * 512:(half + 1) * 512], h_t[:, half * 512:(half + 1) * 512],
                bank[0:64, 0:512]), reads=(bk, "ht"), writes=("ht",))
        for half in range(2):
            def smul(e, half=half):
                sv = S_t[:, half * 512:(half + 1) * 512].rearrange("p (h d) -> p h d", d=128)
                ab = aB[:, ci * H + half * 4: ci * H + half * 4 + 4]
                abb = ab.unsqueeze(2).broadcast_to([128, 4, 128])
                dst = Sscr[:, half * 512:(half + 1) * 512].rearrange("p (h d) -> p h d", d=128)
                return e.tensor_mul(dst, sv, abb)
            B.emit(G_, smul, reads=(f"S{half}", "aB"), writes=(f"Sscr{half}",))
        for half in range(2):
            bank, bk = pool.acquire("f32")
            for hh in range(4):
                h = half * 4 + hh
                B.emit(P_, lambda e, h=h, hh=hh, bank=bank: e.matmul(
                    bank[:, hh * 128:(hh + 1) * 128], kp_t[par][:, h * 128:(h + 1) * 128],
                    vn_t[:, h * 128:(h + 1) * 128], start=(hh == 0), stop=False),
                    reads=(f"kp{par}", f"vn{h // 4}"), writes=(bk,))
            B.emit(P_, lambda e, half=half, bank=bank: e.matmul(
                bank[:, 0:512], identb[:], Sscr[:, half * 512:(half + 1) * 512],
                start=False, stop=True), reads=("identb", f"Sscr{half}"), writes=(bk,))
            B.emit(A_, lambda e, half=half, bank=bank: e.activation(
                S_t[:, half * 512:(half + 1) * 512], bank[:, 0:512], AF.Copy),
                reads=(bk,), writes=(f"S{half}",))
        B.emit(G_, lambda e: e.tensor_mul(Rs_t[:], h_t[:], h_t[:]), reads=("ht",),
               writes=("Rs0", "Rs1"))
        B.emit(V, lambda e: e.reduce_sum(rst_t[:], Rs_t[:].rearrange("p (h d) -> p h d", d=128),
                                         axis=AX.X), reads=("Rs0", "Rs1"), writes=("rst",))
        B.emit(A_, lambda e: e.activation(rst_t[:], rst_t[:], AF.Ln, scale=1.0 / DV,
                                          bias=eps6[0:64, 0:1]),
               reads=("rst", "eps6"), writes=("rst",))
        B.emit(A_, lambda e: e.activation(rst_t[:], rst_t[:], AF.Exp, scale=-0.5),
               reads=("rst",), writes=("rst",))

        def hgmul(e):
            hv = h_t[:].rearrange("p (h d) -> p h d", d=128)
            rb = rst_t[:].unsqueeze(2).broadcast_to([64, H, 128])
            return e.tensor_mul(Rs_t[:].rearrange("p (h d) -> p h d", d=128), hv, rb)
        B.emit(G_, hgmul, reads=("ht", "rst"), writes=("Rs0", "Rs1"))
        B.emit(V, lambda e: e.tensor_mul(hg_t[c2 % 2][par * 64:(par + 1) * 64, :], Rs_t[:],
                                         z_tm[c2][par][:]),
               reads=("Rs0", "Rs1", f"z{c2}_{par}"), writes=("hg0",))

    def c2_out(s, b, c2):
        hgk = "hg0"
        bankb, bkb = pool.acquire("bf16")
        for h in range(H):
            B.emit(P_, lambda e, h=h, bankb=bankb: e.transpose(
                bankb[:, h * 128:(h + 1) * 128], hg_t[c2 % 2][:, h * 128:(h + 1) * 128], identb[:]),
                reads=(hgk, "identb"), writes=(bkb,))
        B.emit(A_, lambda e, bankb=bankb: e.activation(hgf[:], bankb[:, 0:1024], AF.Copy),
               reads=(bkb,), writes=("hgf",))
        bankW = []
        for ob in range(2):
            bank, bk = pool.acquire("f32")
            bankW.append((bank, bk))
            for h in range(H):
                B.emit(P_, lambda e, h=h, ob=ob, bank=bank: e.matmul(
                    bank[:, 0:512], hgf[:, h * 128:(h + 1) * 128],
                    wo[h][:, ob * 512:(ob + 1) * 512], start=(h == 0), stop=(h == 7)),
                    reads=("hgf", f"wo{h}"), writes=(bk,))
        for half, (bank, bk) in enumerate(bankW):
            B.emit(V, lambda e, half=half, bank=bank: e.tensor_add(
                y1_t[:, half * 512:(half + 1) * 512], bank[:, 0:512],
                xf[c2][:, half * 512:(half + 1) * 512]),
                reads=(bk, f"xf{c2}"), writes=("y1",))
        ln_norm(y1_t, "y1", skeys=("stage", "Rs0", "Rs1"))
        B.emit(V, lambda e: e.reduce_max(stage2[:, 1:2], y1_t[:], axis=AX.X,
                                         apply_absolute_value=True),
               reads=("y1",), writes=("stage2",))
        B.emit(V, lambda e: e.tensor_max(amax_col[:, 0:1], amax_col[:, 0:1], stage2[:, 1:2]),
               reads=("stage2", "amax"), writes=("amax",))
        r0 = s * T + b * BLK + c2 * 128
        r = dma(G_, y1sp_d[r0:r0 + 128, :], y1_t[:], (), rkeys=("y1",))
        if phase == "attn":
            out_refs.append(r)
        dbg_store("y1", y1_t[:], ("y1",), row0=r0)

    def ln_norm(tile, key, scr=None, s2=None, skeys=("stage",), s2key="stage2", eng=V):
        if scr is None: scr = stage
        if s2 is None: s2 = stage2
        B.emit(A_, lambda e: e.activation(scr[:, 0:D], tile[:], AF.Copy,
                                          accum_out=s2[:, 2:3]),
               reads=(key,), writes=(*skeys, s2key))
        B.emit(A_, lambda e: e.activation(scr[:, 0:D], tile[:], AF.Square,
                                          accum_out=s2[:, 3:4]),
               reads=(key,), writes=(*skeys, s2key))
        B.emit(eng, lambda e: e.tensor_scalar(s2[:, 4:5], s2[:, 2:3], 1.0 / D, None,
                                            ALU.mult), reads=(s2key,), writes=(s2key,))
        B.emit(eng, lambda e: e.tensor_mul(s2[:, 5:6], s2[:, 4:5], s2[:, 4:5]),
               reads=(s2key,), writes=(s2key,))
        B.emit(eng, lambda e: e.tensor_scalar(s2[:, 6:7], s2[:, 3:4], 1.0 / D, None,
                                            ALU.mult), reads=(s2key,), writes=(s2key,))
        B.emit(eng, lambda e: e.tensor_sub(s2[:, 6:7], s2[:, 6:7], s2[:, 5:6]),
               reads=(s2key,), writes=(s2key,))
        B.emit(A_, lambda e: e.activation(s2[:, 7:8], s2[:, 6:7], AF.Ln,
                                          bias=eps5[:, 0:1]),
               reads=(s2key, "eps5"), writes=(s2key,))
        B.emit(A_, lambda e: e.activation(s2[:, 8:9], s2[:, 7:8], AF.Exp, scale=-0.5),
               reads=(s2key,), writes=(s2key,))
        B.emit(eng, lambda e: e.tensor_scalar(tile[:], tile[:], s2[:, 4:5], s2[:, 8:9],
                                            ALU.subtract, ALU.mult),
               reads=(key, s2key), writes=(key,))

    def ffn_phase():
        B.global_barrier()
        attn_stack.close()
        _scope[0] = st
        biasB = sb("biasB", [128, D], F32)
        fbrow = sb("fbrow", [1, D], F32)
        wq8 = [sb(f"wq8_{c}", [128, D], FP8) for c in range(8)]
        x8 = sb("x8", [128, D], FP8)
        fscr = sb("fscr", [128, 512], F32)
        NFB = 8
        fy = [sb(f"fy{i}", [128, D], F32) for i in range(NFB)]
        fsc = [sb(f"fsc{i}", [128, D], F32) for i in range(NFB)]
        fs2 = [sb(f"fs2{i}", [128, 16], F32) for i in range(NFB)]
        fx8 = [sb(f"fx8{i}", [128, D], FP8) for i in range(NFB)]
        dma(SY, fbrow[:], fb_d.rearrange("(one d) -> one d", one=1), ("fbrow",))
        for half in range(2):
            bank2, bk2 = pool.acquire("f32")
            B.emit(P_, lambda e, half=half, bank2=bank2: e.matmul(
                bank2[:, 0:512], ones1r[:], fbrow[0:1, half * 512:(half + 1) * 512],
                start=True, stop=True), reads=("ones1r", "fbrow"), writes=(bk2,))
            B.emit(V, lambda e, half=half, bank2=bank2: e.tensor_copy(
                biasB[:, half * 512:(half + 1) * 512], bank2[:, 0:512]),
                reads=(bk2,), writes=("biasB",))
        if phase != "ffn":
            bank, bk = pool.acquire("f32")
            B.emit(P_, lambda e, bank=bank: e.transpose(bank[0:1, 0:128], amax_col[:, 0:1],
                                                        identf[:]),
                   reads=("amax", "identf"), writes=(bk,))
            B.emit(V, lambda e, bank=bank: e.reduce_max(gsc[0:1, 0:1], bank[0:1, 0:128], axis=AX.X),
                   reads=(bk,), writes=("gsc",))
        B.emit(V, lambda e: e.memset(amax_col[:], 0.0), writes=("amax",))
        for c in range(8):
            bt, bkk = fy[c % NFB], f"fy{c % NFB}"
            dma(G_ if c % 2 else SY, bt[:], fw_d[c * 128:(c + 1) * 128, :], (bkk,))
            B.emit(V, lambda e, bt=bt, c=c, bkk=bkk: e.reduce_max(
                fs2[c % NFB][:, 0:1], bt[:], axis=AX.X, apply_absolute_value=True),
                   reads=(bkk,), writes=(f"fs2{c % NFB}",))
            B.emit(V, lambda e, c=c: e.tensor_max(amax_col[:, 0:1], amax_col[:, 0:1],
                                             fs2[c % NFB][:, 0:1]),
                   reads=(f"fs2{c % NFB}", "amax"), writes=("amax",))
        bank, bk = pool.acquire("f32")
        B.emit(P_, lambda e, bank=bank: e.transpose(bank[0:1, 0:128], amax_col[:, 0:1],
                                                    identf[:]),
               reads=("amax", "identf"), writes=(bk,))
        B.emit(V, lambda e, bank=bank: e.reduce_max(gsc[0:1, 2:3], bank[0:1, 0:128], axis=AX.X),
               reads=(bk,), writes=("gsc",))
        if phase == "ffn":
            dma(SY, stage2[0:1, 0:8], amax8_d[0:1, 0:8], ("stage2",))
            B.emit(V, lambda e: e.reduce_max(gsc[0:1, 1:2], stage2[0:1, 0:8], axis=AX.X),
                   reads=("stage2",), writes=("gsc",))
        elif do_coll and n_cores > 1:
            dma(SY, cci_d[0:1, 0:8], gsc[0:1, 0:8], ("cci",), rkeys=("gsc",))
            B.emit(G_, lambda e: e.collective_compute(
                "AllReduce", ALU.max, [list(range(n_cores))], [cci_d[0:1, 0:8]],
                [cco_d[0:1, 0:8]]), reads=("cci",), writes=("cco",), dma=True)
            dma(SY, gsc[0:1, 1:2], cco_d[0:1, 0:1], ("gsc",), rkeys=("cco",))
        else:
            B.emit(V, lambda e: e.tensor_copy(gsc[0:1, 1:2], gsc[0:1, 0:1]),
                   reads=("gsc",), writes=("gsc",))
        B.emit(V, lambda e: e.reciprocal(gsc[0:1, 3:4], gsc[0:1, 1:2]), reads=("gsc",), writes=("gsc",))
        B.emit(V, lambda e: e.tensor_scalar(gsc[0:1, 4:5], gsc[0:1, 3:4], 224.0, None, ALU.mult),
               reads=("gsc",), writes=("gsc",))
        B.emit(V, lambda e: e.reciprocal(gsc[0:1, 5:6], gsc[0:1, 2:3]), reads=("gsc",), writes=("gsc",))
        B.emit(V, lambda e: e.tensor_scalar(gsc[0:1, 6:7], gsc[0:1, 5:6], 224.0, None, ALU.mult),
               reads=("gsc",), writes=("gsc",))
        B.emit(V, lambda e: e.tensor_mul(gsc[0:1, 7:8], gsc[0:1, 1:2], gsc[0:1, 2:3]),
               reads=("gsc",), writes=("gsc",))
        B.emit(V, lambda e: e.tensor_scalar(gsc[0:1, 7:8], gsc[0:1, 7:8], 1.0 / (224.0 * 224.0),
                                            None, ALU.mult), reads=("gsc",), writes=("gsc",))
        for col, tile, tkey in ((4, sx_col, "sxcol"), (6, sw_col, "swcol"), (7, fs_col, "fscol")):
            bank2, bk2 = pool.acquire("f32")
            B.emit(P_, lambda e, col=col, bank2=bank2: e.matmul(
                bank2[:, 0:1], ones1r[:], gsc[0:1, col:col + 1], start=True, stop=True),
                reads=("gsc", "ones1r"), writes=(bk2,))
            B.emit(V, lambda e, tile=tile, bank2=bank2: e.tensor_copy(tile[:], bank2[:, 0:1]),
                   reads=(bk2,), writes=(tkey,))
        for c in range(8):
            bt, bkk = fy[c % NFB], f"fy{c % NFB}"
            fsr, fsk = fsc[c % NFB], f"fsc{c % NFB}"
            dma(G_ if c % 2 else SY, bt[:], fw_d[c * 128:(c + 1) * 128, :], (bkk,))
            for q in range(2):
                bank2, bk2 = pool.acquire("f32")
                for j in range(4):
                    ic = q * 4 + j
                    B.emit(P_, lambda e, ic=ic, j=j, bank2=bank2, bt=bt: e.transpose(
                        bank2[:, j * 128:(j + 1) * 128], bt[:, ic * 128:(ic + 1) * 128],
                        identf[:]), reads=(bkk, "identf"), writes=(bk2,))
                B.emit(V, lambda e, bank2=bank2, fsr=fsr, q=q: e.tensor_scalar(
                    fsr[:, q * 512:(q + 1) * 512], bank2[:, 0:512], sw_col[:, 0:1], None,
                    ALU.mult), reads=(bk2, "swcol"), writes=(fsk,))
                for j in range(4):
                    ic = q * 4 + j
                    B.emit(V, lambda e, ic=ic, j=j, c=c, fsr=fsr, q=q: e.tensor_copy(
                        wq8[ic][:, c * 128:(c + 1) * 128],
                        fsr[:, q * 512 + j * 128: q * 512 + (j + 1) * 128]),
                        reads=(fsk,), writes=(f"wq8_{ic}",))
        for s in range(n_seq):
            for tt in range(T // 128):
                ib = (s * (T // 128) + tt) % NFB
                yt, sc, s2, xq = fy[ib], fsc[ib], fs2[ib], fx8[ib]
                ky, ks, k2, kx = f"fy{ib}", f"fsc{ib}", f"fs2{ib}", f"fx8{ib}"
                r0 = s * T + tt * 128
                dma(G_, yt[:], (y1in_d if phase == "ffn" else y1sp_d)[r0:r0 + 128, :],
                    (ky,))
                B.emit(A_, lambda e, yt=yt, sc=sc: e.activation(
                    sc[:, 0:D], yt[:], AF.Copy, scale=sx_col[:, 0:1]),
                       reads=(ky, "sxcol"), writes=(ks,))
                for q in range(2):
                    bank2, bk2 = pool.acquire("f32")
                    for j in range(4):
                        ic = q * 4 + j
                        B.emit(P_, lambda e, ic=ic, j=j, bank2=bank2, sc=sc: e.transpose(
                            bank2[:, j * 128:(j + 1) * 128], sc[:, ic * 128:(ic + 1) * 128],
                            identf[:]), reads=(ks, "identf"), writes=(bk2,))
                    B.emit(V, lambda e, q=q, bank2=bank2, xq=xq: e.tensor_copy(
                        xq[:, q * 512:(q + 1) * 512], bank2[:, 0:512]),
                        reads=(bk2,), writes=(kx,))
                bankF = []
                for ob in range(2):
                    bank, bk = pool.acquire("f32")
                    bankF.append((bank, bk))
                    for c in range(8):
                        B.emit(P_, lambda e, c=c, ob=ob, bank=bank, xq=xq: e.matmul(
                            bank[:, 0:512], xq[:, c * 128:(c + 1) * 128],
                            wq8[c][:, ob * 512:(ob + 1) * 512], start=(c == 0), stop=(c == 7)),
                            reads=(kx, f"wq8_{c}"), writes=(bk,))
                for half, (bank, bk) in enumerate(bankF):
                    B.emit(V, lambda e, half=half, bank=bank, sc=sc: e.scalar_tensor_tensor(
                        sc[:, half * 512:(half + 1) * 512], bank[:, 0:512], fs_col[:, 0:1],
                        biasB[:, half * 512:(half + 1) * 512], ALU.mult, ALU.add),
                        reads=(bk, "fscol", "biasB"), writes=(ks,))
                B.emit(V, lambda e, yt=yt, sc=sc: e.tensor_add(yt[:], sc[:, 0:D], yt[:]),
                       reads=(ks, ky), writes=(ky,))
                ln_norm(yt, ky, scr=sc, s2=s2, skeys=(ks,), s2key=k2)
                r = dma(SY, y_d[r0:r0 + 128, :], yt[:], (), rkeys=(ky,))
                out_refs.append(r)

    if phase != "ffn":
        phase_w()
    else:
        dma(SY, identf[:], id_d, ("identf",))
        dma(G_, identf2[:, 0:128], id_d, ("identf2",))
        dma(G_, identf2[:, 128:256], id_d, ("identf2",))
        B.emit(V, lambda e: e.memset(ones1r[:], 1.0), writes=("ones1r",))
        B.emit(V, lambda e: e.memset(eps5[:], 1e-5), writes=("eps5",))
    gblk = 0
    for s in range(n_seq if phase != "ffn" else 0):
        seq_init(s)
        for b in range(n_blk):
            load_transpose_x(s, b)
            projections(s, b)
            if upto == "proj":
                for ti in range(24):
                    dbg_store("proj", qkv[ti][:, 3:3 + BLK], (f"qkv{ti}",), row0=ti * 128)
                dbg_store("zdbg", z_tm[0][:], ("z0",), row0=0)
                dbg_store("bab", ar("ba_b"), (ark("ba_b"),), row0=0)
                break
            conv(s, b)
            gchain(s, b)
            l2norm(s, b)
            if upto == "conv":
                for ti in range(24):
                    dbg_store("convo", qkv[ti][:, 3:3 + BLK], (f"qkv{ti}",), row0=ti * 128)
                dbg_store("cum", ar("cum"), (ark("cum"),), row0=0)
                dbg_store("rq", ar("rq"), (ark("rq"),), row0=0)
                dbg_store("beta", ar("beta"), (ark("beta"),), row0=0)
                break
            scalerows(s, b, gblk % 2)
            for h in range(H):
                prescan_head(s, b, h, (gblk * H + h) % NRS)
                if h % 4 == 3:
                    factor_chains(s, b, [(hh, hh % NASET, hh % 4)
                                         for hh in range(h - 3, h + 1)])

            if upto == "prescan":
                dbg_store("TmT0", TmT[0][0][0][:], ("TmT0_0_0",))
                dbg_store("TmT0b", TmT[0][0][1][:], ("TmT0_0_1",))
                dbg_store("TmT7", TmT[7][0][0][:], ("TmT7_0_0",))
                dbg_store("attnT0", attnT[0][0][:], ("aT0_0",))
                dbg_store("attnT7", attnT[7][0][:], ("aT7_0",))
                break
            for ci in range(nb_ch):
                par, c2 = ci % 2, ci // 2
                if par == 0:
                    c2_prep(s, b, c2)
                scan_step(s, b, ci)
                if par == 1:
                    c2_out(s, b, c2)
            if upto == "scan" and b == 0:
                dbg_store("hg0", hg_t[0][:], ("hg0",))
                break
            gblk += 1
        if upto in ("proj", "conv", "prescan", "scan"):
            break
    if phase == "attn":
        bank, bk = pool.acquire("f32")
        B.emit(P_, lambda e, bank=bank: e.transpose(bank[0:1, 0:128], amax_col[:, 0:1], identf[:]),
               reads=("amax", "identf"), writes=(bk,))
        B.emit(V, lambda e, bank=bank: e.reduce_max(gsc[0:1, 0:1], bank[0:1, 0:128], axis=AX.X),
               reads=(bk,), writes=("gsc",))
        B.emit(V, lambda e: e.tensor_copy(gsc[0:1, 1:8], gsc[0:1, 0:1].broadcast_to([1, 7])),
               reads=("gsc",), writes=("gsc",))
        r = dma(SY, amax_d[0:1, 0:8], gsc[0:1, 0:8], (), rkeys=("gsc",))
        out_refs.append(r)
    if upto == "full" and phase in ("full", "ffn"):
        ffn_phase()
    B.finalize(final_waits=out_refs if out_refs else B.barrier_ref(["y1"]))
    attn_stack.close()
    st.close()
    return nc


def host_consts(nb_ch=8):
    ident = np.eye(128, dtype=np.float32)
    i, j = np.mgrid[0:64, 0:64]
    NEG = np.float32(-1e30)
    m_ls = np.where(i > j, 0.0, NEG).astype(np.float32)
    m_ui = np.where(j >= i, 0.0, NEG).astype(np.float32)
    m_us = np.where(j > i, 0.0, NEG).astype(np.float32)
    tile8 = lambda m: np.tile(m, (1, 8)).astype(np.float32)
    eh = np.zeros((128, 64), np.float32)
    for h in range(8):
        eh[:, h * 8 + h] = 0.0
    eh = np.zeros((128, 64), np.float32)
    for h in range(8):
        eh[:, h * 8 + h] = 1.0
    e8 = np.zeros((8, 8 * 8), np.float32)
    for c in range(8):
        for h in range(8):
            e8[h, c * 8 + h] = 1.0
    lrp = np.zeros((16, 512), np.float32)
    lrp[1, :] = 1.0
    rrp = np.zeros((4, 3, 8, 64), np.float32)
    rrp[0, 0] = 1.0; rrp[3, 0] = 1.0
    rrp[0, 1] = -1.0; rrp[2, 1] = 1.0
    rrp[0, 2] = -1.0; rrp[2, 2] = 1.0
    rrp = np.ascontiguousarray(rrp.transpose(0, 2, 1, 3).reshape(4, 1536))
    mall = np.zeros((64, 8, 3, 64), np.float32)
    for ci in range(8):
        mall[:, ci, 0] = m_ls; mall[:, ci, 1] = m_us; mall[:, ci, 2] = m_ui
    mall = np.ascontiguousarray(mall.reshape(64, 1536))
    return dict(ident=ident, mask_all=mall, rr_pat=rrp,
                eh_sel=eh, e8_sel=e8, lr_pat=lrp)


def permute_weights(W_qkvz, W_ba):
    cols = []
    for h in range(8):
        for part in range(3):
            cols.extend(range(h * 512 + part * 128, h * 512 + (part + 1) * 128))
    for h in range(8):
        cols.extend(range(h * 512 + 384, h * 512 + 512))
    Wq = np.ascontiguousarray(W_qkvz[:, cols])
    bcols = [2 * h for h in range(8)] + [2 * h + 1 for h in range(8)]
    Wb = np.ascontiguousarray(W_ba[:, bcols])
    return Wq, Wb


from concourse.bass_utils import run_bass_kernel_spmd

_CFG = dict(n_seq=2, T=2048, BLK=256, upto="full", collective=False, n_cores=8,
            two_launch=True)

def kernel(L0_attn_x, L0_attn_W_qkvz, L0_attn_W_ba, L0_attn_conv_w, L0_attn_dt_bias,
           L0_attn_A_log, L0_attn_norm_weight, L0_attn_W_out, L0_ffn_weight, L0_ffn_bias):
    n_cores = 8
    cfg = dict(_CFG)
    x = np.ascontiguousarray(np.asarray(L0_attn_x, np.float32))
    B_, T_, D_ = x.shape
    assert (B_, T_, D_) == (16, 2048, 1024)
    Wq, Wb = permute_weights(np.asarray(L0_attn_W_qkvz, np.float32),
                             np.asarray(L0_attn_W_ba, np.float32))
    consts = host_consts()
    shared = dict(wqkvz=Wq, wba=Wb,
                  convw=np.ascontiguousarray(np.asarray(L0_attn_conv_w, np.float32).reshape(3072, 4)),
                  dtb=np.ascontiguousarray(np.asarray(L0_attn_dt_bias, np.float32)),
                  alog=np.ascontiguousarray(np.asarray(L0_attn_A_log, np.float32)),
                  nw=np.ascontiguousarray(np.asarray(L0_attn_norm_weight, np.float32)),
                  wout=np.ascontiguousarray(np.asarray(L0_attn_W_out, np.float32)),
                  ffnw=np.ascontiguousarray(np.asarray(L0_ffn_weight, np.float32)),
                  ffnb=np.ascontiguousarray(np.asarray(L0_ffn_bias, np.float32)))
    shared.update(consts)
    if cfg.get("two_launch"):
        cfg1 = dict(cfg); cfg1["phase"] = "attn"
        nc1 = build(cfg1)
        in_maps = []
        for c in range(n_cores):
            m = dict(shared)
            m["x"] = np.ascontiguousarray(x[2 * c: 2 * c + 2].reshape(2 * 2048, 1024))
            in_maps.append(m)
        res1 = run_bass_kernel_spmd(nc1, in_maps, core_ids=list(range(n_cores)))
        amax8 = np.ascontiguousarray(
            np.concatenate([np.asarray(res1.results[c]["amaxout"])[0:1, 0:1]
                            for c in range(n_cores)], axis=1).astype(np.float32))
        cfg2 = dict(cfg); cfg2["phase"] = "ffn"
        nc2 = build(cfg2)
        in_maps2 = []
        for c in range(n_cores):
            m = dict(shared)
            m["x"] = np.zeros((2 * 2048, 1024), np.float32)
            m["y1in"] = np.ascontiguousarray(np.asarray(res1.results[c]["y1out"], np.float32))
            m["amax8"] = amax8
            in_maps2.append(m)
        res = run_bass_kernel_spmd(nc2, in_maps2, core_ids=list(range(n_cores)))
    else:
        nc = build(cfg)
        in_maps = []
        for c in range(n_cores):
            m = dict(shared)
            m["x"] = np.ascontiguousarray(x[2 * c: 2 * c + 2].reshape(2 * 2048, 1024))
            in_maps.append(m)
        res = run_bass_kernel_spmd(nc, in_maps, core_ids=list(range(n_cores)))
    out = np.concatenate([res.results[c]["y"].reshape(2, 2048, 1024) for c in range(n_cores)],
                         axis=0)
    return np.ascontiguousarray(out.astype(np.float32))


# version 27
# speedup vs baseline: 1.0348x; 1.0348x over previous
import numpy as np
import concourse.bass as bass
import concourse.mybir as mybir

F32 = mybir.dt.float32
BF16 = mybir.dt.bfloat16
FP8 = mybir.dt.float8e4
AF = mybir.ActivationFunctionType
ALU = mybir.AluOpType
AX = mybir.AxisListType

ENGINES = ("tensor", "vector", "scalar", "gpsimd", "sync")


class OpRef:
    __slots__ = ("engine", "sem_idx", "target")
    def __init__(self, engine, sem_idx, target):
        self.engine = engine
        self.sem_idx = sem_idx
        self.target = target


class Builder:
    def __init__(self, nc):
        self.nc = nc
        self.prog = {e: [] for e in ENGINES}
        self.sem_names = []
        self.sem_counts = {}
        self.eng_sem = {}
        self.dma_sems = {}
        self.dma_rr = {}
        for e in ENGINES:
            self.eng_sem[e] = self._new_sem(f"c_{e}")
        for e, nr in (("sync", 12), ("gpsimd", 8), ("scalar", 4), ("vector", 4)):
            self.dma_sems[e] = [self._new_sem(f"d_{e}{i}") for i in range(nr)]
            self.dma_rr[e] = 0
        self.tiles = {}
        self.water = {}
        self.n_ops = 0

    def _new_sem(self, name):
        idx = len(self.sem_names)
        self.sem_names.append(name)
        self.sem_counts[idx] = 0
        return idx

    def _deps_for(self, reads, writes):
        deps = []
        for k in reads:
            t = self.tiles.get(k)
            if t and t["writer"] is not None:
                deps.append(t["writer"])
        for k in writes:
            t = self.tiles.get(k)
            if t:
                if t["writer"] is not None:
                    deps.append(t["writer"])
                deps.extend(t["readers"])
        return deps

    def emit(self, engine, fn, reads=(), writes=(), dma=False, dma_inc=None, extra_deps=()):
        deps = self._deps_for(reads, writes)
        deps.extend(extra_deps)
        bar = getattr(self, "_barrier", None)
        if bar is not None and bar.get(engine):
            deps.extend(d for d in bar[engine] if d.engine != engine or True)
            bar[engine] = []
        waits = []
        best = {}
        for d in deps:
            if d is None:
                continue
            if engine == "tensor" and d.engine == "tensor" and not dma:
                continue
            cur = best.get(d.sem_idx, 0)
            if d.target > cur:
                best[d.sem_idx] = d.target
        for semidx, val in best.items():
            wkey = (engine, semidx)
            if self.water.get(wkey, 0) >= val:
                continue
            self.water[wkey] = val
            waits.append((semidx, val))
        if dma:
            sems = self.dma_sems[engine]
            semidx = sems[self.dma_rr[engine] % len(sems)]
            self.dma_rr[engine] += 1
            inc = 16 if dma_inc is None else dma_inc
            prior = self.sem_counts[semidx]
            if prior > 0:
                wkey = (engine, semidx)
                if self.water.get(wkey, 0) < prior:
                    self.water[wkey] = prior
                    waits.append((semidx, prior))
        else:
            semidx = self.eng_sem[engine]
            inc = 1
        self.sem_counts[semidx] += inc
        target = self.sem_counts[semidx]
        ref = OpRef(engine, semidx, target)
        self.prog[engine].append((fn, waits, [(semidx, inc)]))
        for k in reads:
            self.tiles.setdefault(k, {"writer": None, "readers": []})["readers"].append(ref)
        for k in writes:
            t = self.tiles.setdefault(k, {"writer": None, "readers": []})
            t["writer"] = ref
            t["readers"] = []
        self.n_ops += 1
        return ref

    def global_barrier(self):
        refs = []
        for e in ENGINES:
            semidx = self.eng_sem[e]
            if self.sem_counts[semidx] > 0:
                refs.append(OpRef(e, semidx, self.sem_counts[semidx]))
            for semidx in self.dma_sems.get(e, ()):
                if self.sem_counts[semidx] > 0:
                    refs.append(OpRef(e, semidx, self.sem_counts[semidx]))
        self._barrier = {e: list(refs) for e in ENGINES}

    def barrier_ref(self, keys):
        out = []
        for k in keys:
            t = self.tiles.get(k)
            if t:
                if t["writer"]:
                    out.append(t["writer"])
                out.extend(t["readers"])
        return out

    def finalize(self, final_waits=()):
        nc = self.nc
        sems = []
        import contextlib
        stack = contextlib.ExitStack()
        for name in self.sem_names:
            sems.append(stack.enter_context(nc.semaphore(name)))
        blk = stack.enter_context(nc.Block())
        def runner(eng_name):
            def run(eng):
                for fn, waits, incs in self.prog[eng_name]:
                    for semidx, val in waits:
                        eng.wait_ge(sems[semidx], val)
                    ins = fn(eng)
                    for semidx, inc in incs:
                        ins.then_inc(sems[semidx], inc)
                if eng_name == "sync":
                    best = {}
                    for d in final_waits:
                        if d.target > best.get(d.sem_idx, 0):
                            best[d.sem_idx] = d.target
                    for semidx, val in best.items():
                        eng.wait_ge(sems[semidx], val)
            return run
        blk.tensor(runner("tensor"))
        blk.vector(runner("vector"))
        blk.scalar(runner("scalar"))
        blk.gpsimd(runner("gpsimd"))
        blk.sync(runner("sync"))
        stack.close()


class PsumPool:
    def __init__(self, nc, stack, n_f32=5, n_bf16=3):
        self.banks = []
        for i in range(n_f32):
            t = stack.enter_context(nc.psum_tensor(f"psf{i}", [128, 512], F32))
            self.banks.append(("f32", t, f"psum_f{i}"))
        for i in range(n_bf16):
            t = stack.enter_context(nc.psum_tensor(f"psb{i}", [128, 1024], BF16))
            self.banks.append(("bf16", t, f"psum_b{i}"))
        self.order = {"f32": [i for i, b in enumerate(self.banks) if b[0] == "f32"],
                      "bf16": [i for i, b in enumerate(self.banks) if b[0] == "bf16"]}
        self.rr = {"f32": 0, "bf16": 0}

    def acquire(self, dtype="f32"):
        lst = self.order[dtype]
        i = lst[self.rr[dtype] % len(lst)]
        self.rr[dtype] += 1
        kind, t, key = self.banks[i]
        return t, key


"""Deltanet megakernel program builder (per-core SPMD program).
Sequences sequential. fm = [feat x tok], tm = [tok x feat]. Chunks 64; c2 = 128-tok pair.
"""
import numpy as np
import contextlib
import concourse.bass as bass
import concourse.mybir as mybir

H = 8; DK = 128; DV = 128; CH = 64; D = 1024
DKS = float(DK) ** -0.5
V, A_, P_, G_, SY = "vector", "scalar", "tensor", "gpsimd", "sync"


def build(cfg):
    n_seq = cfg["n_seq"]; T = cfg["T"]; BLK = cfg["BLK"]
    n_blk = T // BLK
    nb_ch = BLK // CH
    nb_c2 = BLK // 128
    upto = cfg.get("upto", "full")
    phase = cfg.get("phase", "full")
    dbg = dict(cfg.get("debug", ()))
    do_coll = cfg.get("collective", True)
    n_cores = cfg.get("n_cores", 8)

    nc = bass.Bass("TRN2", target_bir_lowering=False)
    dt = nc.dram_tensor
    x_d = dt("x", [n_seq * T, D], F32, kind="ExternalInput").ap()
    wq_d = dt("wqkvz", [D, 4096], F32, kind="ExternalInput").ap()
    wba_d = dt("wba", [D, 16], F32, kind="ExternalInput").ap()
    cw_d = dt("convw", [3072, 4], F32, kind="ExternalInput").ap()
    dtb_d = dt("dtb", [H], F32, kind="ExternalInput").ap()
    alog_d = dt("alog", [H], F32, kind="ExternalInput").ap()
    nw_d = dt("nw", [DV], F32, kind="ExternalInput").ap()
    wo_d = dt("wout", [D, D], F32, kind="ExternalInput").ap()
    fw_d = dt("ffnw", [D, D], F32, kind="ExternalInput").ap()
    fb_d = dt("ffnb", [D], F32, kind="ExternalInput").ap()
    id_d = dt("ident", [128, 128], F32, kind="ExternalInput").ap()
    mall_d = dt("mask_all", [64, 1536], F32, kind="ExternalInput").ap()
    eh_d = dt("eh_sel", [128, 64], F32, kind="ExternalInput").ap()
    lrp_d = dt("lr_pat", [16, 512], F32, kind="ExternalInput").ap()
    rrp_d = dt("rr_pat", [4, 1536], F32, kind="ExternalInput").ap()
    e8_d = dt("e8_sel", [H, 8 * H], F32, kind="ExternalInput").ap()
    y_d = dt("y", [n_seq * T, D], F32, kind="ExternalOutput").ap()
    if phase == "attn":
        y1sp_d = dt("y1out", [n_seq * T, D], F32, kind="ExternalOutput").ap()
        amax_d = dt("amaxout", [1, 8], F32, kind="ExternalOutput").ap()
    else:
        y1sp_d = dt("y1spill", [n_seq * T, D], F32, kind="Internal").ap()
    if phase == "ffn":
        y1in_d = dt("y1in", [n_seq * T, D], F32, kind="ExternalInput").ap()
        amax8_d = dt("amax8", [1, 8], F32, kind="ExternalInput").ap()
    cci_d = dt("cc_in", [1, 8], F32, kind="Internal").ap()
    cco_d = dt("cc_out", [1, 8], F32, kind="Internal").ap()
    dbg_d = {k: dt("dbg_" + k, list(v), F32, kind="ExternalOutput").ap() for k, v in dbg.items()}

    B = Builder(nc)
    st = contextlib.ExitStack()
    attn_stack = contextlib.ExitStack()
    _scope = [attn_stack]
    sb = lambda name, shape, dtype: _scope[0].enter_context(nc.sbuf_tensor(name, shape, dtype))
    pool = PsumPool(nc, st, n_f32=5, n_bf16=3)

    _scope[0] = st
    identf = sb("identf", [128, 128], F32)
    one_col = sb("onecol", [128, 1], F32)
    eps5 = sb("eps5", [128, 1], F32)
    eps6 = sb("eps6", [128, 1], F32)
    ones1r = sb("ones1r", [1, 128], F32)
    stage = sb("stage", [128, 1024], F32)
    stage2 = sb("stage2", [128, 16], F32)
    y1_t = sb("y1", [128, D], F32)
    amax_col = sb("amaxcol", [128, 1], F32)
    gsc = sb("gsc", [1, 16], F32)
    sx_col = sb("sxcol", [128, 1], F32)
    fs_col = sb("fscol", [128, 1], F32)
    sw_col = sb("swcol", [128, 1], F32)
    _scope[0] = attn_stack
    wq = [sb(f"wq{c}", [128, 4096], BF16) for c in range(8)]
    wba = [sb(f"wba{c}", [128, 16], BF16) for c in range(8)]
    wo = [sb(f"wo{c}", [128, D], BF16) for c in range(8)]
    nw_col = sb("nwcol", [128, 1], F32)
    identb = sb("identb", [128, 128], BF16)
    mall = sb("mall", [64, nb_ch * 192], BF16)
    ehs = sb("ehs", [128, 64], BF16)
    e8s = sb("e8s", [H, nb_ch * H], BF16)
    ones8 = sb("ones8", [H, 128], F32)
    dt_col = sb("dtcol", [H, 1], F32)
    alog_col = sb("alogcol", [H, 1], F32)
    nega_col = sb("negacol", [H, 1], F32)
    cw_sb = sb("cw", [128, 96], F32)
    dg = [sb(f"dg{k}", [128, 128], BF16) for k in range(8)]
    S_t = sb("S", [128, H * DV], BF16)
    xf = [sb(f"xf{i}", [128, D], F32) for i in range(nb_c2)]
    qkv = [sb(f"qkv{i}", [128, BLK + 4], BF16) for i in range(24)]
    hist = [sb(f"hist{i}", [128, 4], BF16) for i in range(24)]
    xfm = [sb(f"xfm{c}", [128, BLK], BF16) for c in range(8)]
    z_tm = [[sb(f"z{i}_{p}", [64, D], BF16) for p in range(2)] for i in range(nb_c2)]
    AR_NAMES = ["ba_b", "ba_a", "beta", "g", "cum", "eg", "egl", "sqg", "skgN", "skl",
                "lnbrk", "lnrk", "m1", "m2", "m3", "rk", "rq", "alpha", "scr"]
    arena = {nm: sb(f"arn_{nm}", [H, BLK], F32) for nm in AR_NAMES}
    def ar(nm, rows=H):
        return arena[nm][0:rows, :]
    def ark(nm):
        return f"ar_{nm}"
    sc2 = [sb(f"sc2_{i}", [128, BLK], BF16) for i in range(2)]
    NRS = 2
    Lrow = [sb(f"Lr{i}", [4, BLK], F32) for i in range(NRS)]
    Rr = [sb(f"Rr{i}", [4, nb_ch * 192], F32) for i in range(NRS)]
    Dall = [sb(f"Dall{i}", [64, nb_ch * 192], BF16) for i in range(NRS)]
    NASET = 4
    NFSET = 4
    Ab = [sb(f"Ab{i}", [128, 256], BF16) for i in range(NASET)]
    ATb = [sb(f"ATb{i}", [128, 256], BF16) for i in range(NASET)]
    attnT = [[sb(f"aT{h}_{k}", [64, 128], BF16) for k in range(nb_c2)] for h in range(H)]
    NSLOT = 6
    fac = [[sb(f"fs{s}_{i}", [128, 256], BF16) for s in range(NSLOT)]
           for i in range(NFSET)]
    identf2 = sb("identf2", [128, 256], BF16)
    TmT = [[[sb(f"TmT{h}_{k}_{p}", [64, 64], BF16) for p in range(2)] for k in range(nb_c2)]
           for h in range(H)]
    kp_t = [sb(f"kp{p}", [64, H * DK], BF16) for p in range(2)]
    vb_t = [sb(f"vb{p}", [64, H * DV], BF16) for p in range(2)]
    scT = sb("scT", [128, 40], F32)
    aB = sb("aB", [128, nb_ch * H], F32)
    R_t = sb("R", [64, H * DV], BF16)
    Rs_t = stage[0:64, 0:H * DV]
    vn_t = sb("vn", [64, H * DV], BF16)
    h_t = sb("ht", [64, H * DV], F32)
    rst_t = sb("rst", [64, H], F32)
    hg_t = [sb("hg0", [128, H * DV], BF16)] * 2
    hgf = sb("hgf", [128, H * DV], BF16)
    Sscr = sb("Sscr", [128, H * DV], BF16)
    ffn_tiles = {}

    def dma(queue, out, in_, wkeys, rkeys=()):
        return B.emit(queue, lambda e, o=out, i=in_: e.dma_start(out=o, in_=i),
                      reads=rkeys, writes=wkeys, dma=True)

    out_refs = []

    def dbg_store(name, src_ap, rkeys, row0=0):
        if name not in dbg_d:
            return
        shp = dbg[name]
        r = B.emit(G_, lambda e: e.dma_start(out=dbg_d[name][row0:row0 + src_ap.shape[0], :],
                                             in_=src_ap),
                   reads=rkeys, writes=(), dma=True)
        out_refs.append(r)

    def phase_w():
        for c in range(8):
            dma(G_, wq[c][:], wq_d[c * 128:(c + 1) * 128, :], (f"wq{c}",))
            dma(G_, wba[c][:], wba_d[c * 128:(c + 1) * 128, :], (f"wba{c}",))
        dma(SY, nw_col[:], nw_d.rearrange("(p one) -> p one", one=1), ("nw",))
        dma(SY, identf[:], id_d, ("identf",))
        dma(G_, identf2[:, 0:128], id_d, ("identf2",))
        dma(G_, identf2[:, 128:256], id_d, ("identf2",))
        dma(G_, identb[:], id_d, ("identb",))
        dma(G_, mall[:], mall_d[:, 0:nb_ch * 192], ("mall",))
        dma(G_, ehs[:], eh_d, ("ehs",))
        dma(G_, e8s[:], e8_d[:, 0:nb_ch * H], ("e8s",))
        dma(SY, dt_col[:], dtb_d.rearrange("(p one) -> p one", one=1), ("dtcol",))
        dma(SY, alog_col[:], alog_d.rearrange("(p one) -> p one", one=1), ("alogcol",))
        dma(SY, cw_sb[:].rearrange("p (a b) -> p a b", a=24),
            cw_d.rearrange("(a p) b -> p a b", p=128), ("cw",))

        for c in range(8):
            dma(SY, stage[:], wo_d[c * 128:(c + 1) * 128, :], ("stage", "Rs0", "Rs1"))
            B.emit(V, lambda e, c=c: e.tensor_scalar(wo[c][:], stage[:], nw_col[:, 0:1],
                                                     None, ALU.mult),
                   reads=("stage", "nw"), writes=(f"wo{c}",))
        B.emit(V, lambda e: e.memset(amax_col[:], 0.0), writes=("amax",))
        B.emit(A_, lambda e: e.activation(nega_col[:], alog_col[:], AF.Exp),
               reads=("alogcol",), writes=("negacol",))
        B.emit(V, lambda e: e.tensor_scalar(nega_col[:], nega_col[:], -1.0, None, ALU.mult),
               reads=("negacol",), writes=("negacol",))
        B.emit(V, lambda e: e.memset(one_col[:], 1.0), writes=("onecol",))
        B.emit(V, lambda e: e.memset(eps6[:], 1e-6), writes=("eps6",))
        B.emit(V, lambda e: e.memset(eps5[:], 1e-5), writes=("eps5",))
        B.emit(V, lambda e: e.memset(ones1r[:], 1.0), writes=("ones1r",))
        B.emit(V, lambda e: e.memset(ones8[:], 1.0), writes=("ones8",))
        for i in range(NASET):
            B.emit(V, lambda e, i=i: e.memset(Ab[i][:], 0.0), writes=(f"Ab{i}",))
            B.emit(V, lambda e, i=i: e.memset(ATb[i][:], 0.0), writes=(f"ATb{i}",))
        for i in range(NRS):
            dma(SY, Lrow[i][:], lrp_d[0:4, 0:BLK], (f"Lr{i}",))
            dma(SY, Rr[i][:], rrp_d[0:4, 0:nb_ch * 192], (f"Rr{i}",))



    def qsl(h, ci):
        return qkv[0 * 8 + h][:, 3 + ci * CH: 3 + (ci + 1) * CH]
    def ksl(h, ci):
        return qkv[1 * 8 + h][:, 3 + ci * CH: 3 + (ci + 1) * CH]
    def qsl2(h, c2):
        return qkv[0 * 8 + h][:, 3 + c2 * 128: 3 + (c2 + 1) * 128]
    def ksl2(h, c2):
        return qkv[1 * 8 + h][:, 3 + c2 * 128: 3 + (c2 + 1) * 128]
    def vsl2(h, c2):
        return qkv[2 * 8 + h][:, 3 + c2 * 128: 3 + (c2 + 1) * 128]

    def seq_init(s):
        B.emit(V, lambda e: e.memset(S_t[:], 0.0), writes=("S0", "S1"))
        for i in range(24):
            B.emit(V, lambda e, i=i: e.memset(qkv[i][:, 0:3], 0.0), writes=(f"qkv{i}",))

    def load_transpose_x(s, b):
        for i in range(nb_c2):
            dma(SY, xf[i][:], x_d[s * T + b * BLK + i * 128: s * T + b * BLK + (i + 1) * 128, :],
                (f"xf{i}",))
        for c in range(8):
            bank, bk = pool.acquire("f32")
            for i in range(nb_c2):
                B.emit(P_, lambda e, c=c, i=i, bank=bank: e.transpose(
                    bank[:, i * 128:(i + 1) * 128], xf[i][:, c * 128:(c + 1) * 128], identf[:]),
                    reads=(f"xf{i}", "identf"), writes=(bk,))
            B.emit(A_, lambda e, c=c, bank=bank: e.activation(xfm[c][:, 0:BLK],
                                                              bank[:, 0:nb_c2 * 128], AF.Copy),
                   reads=(bk,), writes=(f"xfm{c}",))

    def projections(s, b):
        for ot in range(24):
            h, part = ot // 3, ot % 3
            ti = part * 8 + h
            bank, bk = pool.acquire("f32")
            for c in range(8):
                B.emit(P_, lambda e, c=c, h=h, part=part, bank=bank: e.matmul(
                    bank[:, 0:BLK], wq[c][:, h * 384 + part * 128: h * 384 + (part + 1) * 128],
                    xfm[c][:, 0:BLK], start=(c == 0), stop=(c == 7)),
                    reads=(f"wq{c}", f"xfm{c}"), writes=(bk,))
            if b > 0:
                B.emit(V, lambda e, ti=ti: e.tensor_copy(qkv[ti][:, 0:3], hist[ti][:, 0:3]),
                       reads=(f"hist{ti}",), writes=(f"qkv{ti}",))
            B.emit(A_, lambda e, ti=ti, bank=bank: e.activation(qkv[ti][:, 3:3 + BLK],
                                                                bank[:, 0:BLK], AF.Copy),
                   reads=(bk,), writes=(f"qkv{ti}",))
            B.emit(V, lambda e, ti=ti: e.tensor_copy(hist[ti][:, 0:3], qkv[ti][:, BLK:BLK + 3]),
                   reads=(f"qkv{ti}",), writes=(f"hist{ti}",))
        for i in range(nb_c2):
            for zb in range(2):
                bank, bk = pool.acquire("f32")
                for c in range(8):
                    B.emit(P_, lambda e, c=c, i=i, zb=zb, bank=bank: e.matmul(
                        bank[:, 0:512], xfm[c][:, i * 128:(i + 1) * 128],
                        wq[c][:, 3072 + zb * 512: 3072 + (zb + 1) * 512],
                        start=(c == 0), stop=(c == 7)),
                        reads=(f"wq{c}", f"xfm{c}"), writes=(bk,))
                for p in range(2):
                    B.emit(A_, lambda e, i=i, zb=zb, p=p, bank=bank: e.activation(
                        z_tm[i][p][:, zb * 512:(zb + 1) * 512],
                        bank[p * 64:(p + 1) * 64, 0:512], AF.Silu),
                        reads=(bk,), writes=(f"z{i}_{p}",))
        for part, nm in ((0, "ba_b"), (1, "ba_a")):
            bank, bk = pool.acquire("f32")
            for c in range(8):
                B.emit(P_, lambda e, c=c, part=part, bank=bank: e.matmul(
                    bank[0:H, 0:BLK], wba[c][:, part * 8:(part + 1) * 8],
                    xfm[c][:, 0:BLK], start=(c == 0), stop=(c == 7)),
                    reads=(f"wba{c}", f"xfm{c}"), writes=(bk,))
            B.emit(V, lambda e, nm=nm, bank=bank: e.tensor_copy(ar(nm), bank[0:H, 0:BLK]),
                   reads=(bk,), writes=(ark(nm),))

    def conv(s, b):
        for ot in range(24):
            h, part = ot // 3, ot % 3
            ti = part * 8 + h
            for k in range(4):
                dgi = (ot % 2) * 4 + k
                B.emit(V, lambda e, k=k, ti=ti, dgi=dgi: e.tensor_scalar(
                    dg[dgi][:], identb[:], cw_sb[:, ti * 4 + k: ti * 4 + k + 1], None, ALU.mult),
                    reads=("identb", "cw"), writes=(f"dg{dgi}",))
            bank, bk = pool.acquire("f32")
            for k in range(4):
                dgi = (ot % 2) * 4 + k
                B.emit(P_, lambda e, k=k, ti=ti, dgi=dgi, bank=bank: e.matmul(
                    bank[:, 0:BLK], dg[dgi][:], qkv[ti][:, k: k + BLK],
                    start=(k == 0), stop=(k == 3)),
                    reads=(f"dg{dgi}", f"qkv{ti}"), writes=(bk,))
            B.emit(A_, lambda e, ti=ti, bank=bank: e.activation(
                qkv[ti][:, 3:3 + BLK], bank[:, 0:BLK], AF.Silu),
                reads=(bk,), writes=(f"qkv{ti}",))

    def gchain(s, b):
        B.emit(A_, lambda e: e.activation(ar("beta"), ar("ba_b"), AF.Exp, scale=-1.0),
               reads=(ark("ba_b"),), writes=(ark("beta"),))
        B.emit(V, lambda e: e.tensor_scalar(ar("beta"), ar("beta"), 1.0, None, ALU.add),
               reads=(ark("beta"),), writes=(ark("beta"),))
        B.emit(V, lambda e: e.reciprocal(ar("beta"), ar("beta")),
               reads=(ark("beta"),), writes=(ark("beta"),))
        B.emit(V, lambda e: e.tensor_scalar(ar("scr"), ar("ba_a"), dt_col[:, 0:1], 0.0,
                                            ALU.add, ALU.max),
               reads=(ark("ba_a"), "dtcol"), writes=(ark("scr"),))
        B.emit(A_, lambda e: e.activation(ar("g"), ar("ba_a"), AF.Abs, bias=dt_col[:, 0:1]),
               reads=(ark("ba_a"), "dtcol"), writes=(ark("g"),))
        B.emit(A_, lambda e: e.activation(ar("g"), ar("g"), AF.Exp, scale=-1.0),
               reads=(ark("g"),), writes=(ark("g"),))
        B.emit(A_, lambda e: e.activation(ar("g"), ar("g"), AF.Ln, bias=one_col[0:H, 0:1]),
               reads=(ark("g"), "onecol"), writes=(ark("g"),))
        B.emit(V, lambda e: e.tensor_add(ar("g"), ar("g"), ar("scr")),
               reads=(ark("g"), ark("scr")), writes=(ark("g"),))
        B.emit(V, lambda e: e.tensor_scalar(ar("g"), ar("g"), nega_col[:, 0:1], None, ALU.mult),
               reads=(ark("g"), "negacol"), writes=(ark("g"),))
        src, dst = "g", "scr"
        for stp in (1, 2, 4, 8, 16, 32):
            def shift_add(e, stp=stp, src=src, dst=dst):
                a3 = ar(src).rearrange("p (n c) -> p n c", c=CH)
                d3 = ar(dst).rearrange("p (n c) -> p n c", c=CH)
                return e.tensor_add(d3[:, :, stp:], a3[:, :, stp:], a3[:, :, 0:CH - stp])
            def head_copy(e, stp=stp, src=src, dst=dst):
                a3 = ar(src).rearrange("p (n c) -> p n c", c=CH)
                d3 = ar(dst).rearrange("p (n c) -> p n c", c=CH)
                return e.tensor_copy(d3[:, :, 0:stp], a3[:, :, 0:stp])
            B.emit(V, shift_add, reads=(ark(src),), writes=(ark(dst),))
            B.emit(V, head_copy, reads=(ark(src),), writes=(ark(dst),))
            src, dst = dst, src
        if src != "cum":
            B.emit(V, lambda e, src=src: e.tensor_copy(ar("cum"), ar(src)),
                   reads=(ark(src),), writes=(ark("cum"),))
        B.emit(A_, lambda e: e.activation(ar("eg"), ar("cum"), AF.Exp),
               reads=(ark("cum"),), writes=(ark("eg"),))
        def egl_sub(e):
            c3 = ar("cum").rearrange("p (n c) -> p n c", c=CH)
            gl = c3[:, :, CH - 1:CH].broadcast_to([H, nb_ch, CH])
            return e.tensor_sub(ar("egl").rearrange("p (n c) -> p n c", c=CH), gl, c3)
        B.emit(V, egl_sub, reads=(ark("cum"),), writes=(ark("egl"),))
        B.emit(A_, lambda e: e.activation(ar("egl"), ar("egl"), AF.Exp),
               reads=(ark("egl"),), writes=(ark("egl"),))

    def l2norm(s, b):
        for part, nm in ((0, "rq"), (1, "rk")):
            bank, bk = pool.acquire("f32")
            for h in range(H):
                ti = part * 8 + h
                sct, sck = sc2[h % 2], f"sc2_{h % 2}"
                B.emit(V, lambda e, ti=ti, sct=sct: e.tensor_mul(
                    sct[:, 0:BLK], qkv[ti][:, 3:3 + BLK], qkv[ti][:, 3:3 + BLK]),
                       reads=(f"qkv{ti}",), writes=(sck,))
                B.emit(P_, lambda e, h=h, bank=bank, sct=sct: e.matmul(
                    bank[0:H, 0:BLK], ehs[:, h * 8:(h + 1) * 8], sct[:, 0:BLK],
                    start=(h == 0), stop=(h == 7)),
                    reads=("ehs", sck), writes=(bk,))
            B.emit(A_, lambda e, bank=bank: e.activation(ar("scr"), bank[0:H, 0:BLK], AF.Ln,
                                                         bias=eps6[0:H, 0:1]),
                   reads=(bk, "eps6"), writes=(ark("scr"),))
            B.emit(A_, lambda e, nm=nm: e.activation(ar(nm), ar("scr"), AF.Exp, scale=-0.5),
                   reads=(ark("scr"),), writes=(ark(nm),))

    def scalerows(s, b, rset):
        def em(eng, fn, r, w):
            B.emit(eng, fn, reads=tuple(ark(k) if k in arena else k for k in r), writes=(ark(w),))
        em(V, lambda e: e.scalar_tensor_tensor(ar("sqg"), ar("rq"), DKS, ar("eg"),
                                               ALU.mult, ALU.mult), ("rq", "eg"), "sqg")
        em(V, lambda e: e.scalar_tensor_tensor(ar("scr"), ar("rk"), -1.0, ar("beta"),
                                               ALU.mult, ALU.mult), ("rk", "beta"), "scr")
        em(V, lambda e: e.tensor_mul(ar("skgN"), ar("scr"), ar("eg")), ("scr", "eg"), "skgN")
        em(V, lambda e: e.tensor_mul(ar("skl"), ar("rk"), ar("egl")), ("rk", "egl"), "skl")
        em(V, lambda e: e.tensor_mul(ar("scr"), ar("beta"), ar("rk")), ("beta", "rk"), "scr")
        em(A_, lambda e: e.activation(ar("lnbrk"), ar("scr"), AF.Ln), ("scr",), "lnbrk")
        em(A_, lambda e: e.activation(ar("lnrk"), ar("rk"), AF.Ln), ("rk",), "lnrk")
        em(A_, lambda e: e.activation(ar("m3"), ar("rq"), AF.Ln, scale=DKS), ("rq",), "m3")
        em(V, lambda e: e.tensor_sub(ar("m1"), ar("lnrk"), ar("cum")), ("lnrk", "cum"), "m1")
        em(V, lambda e: e.tensor_add(ar("m2"), ar("cum"), ar("lnbrk")), ("cum", "lnbrk"), "m2")
        em(V, lambda e: e.tensor_add(ar("m3"), ar("m3"), ar("cum")), ("m3", "cum"), "m3")
        def acopy(e):
            e3 = ar("eg").rearrange("p (n c) -> p n c", c=CH)
            dst = ar("alpha")[:, 0:nb_ch].unsqueeze(2)
            return e.tensor_copy(dst, e3[:, :, CH - 1:CH])
        B.emit(V, acopy, reads=(ark("eg"),), writes=(ark("alpha"),))
        def e8mul(e):
            al = ar("alpha")[:, 0:nb_ch]
            alb = al.unsqueeze(2).broadcast_to([H, nb_ch, H])
            e8v = e8s[:].rearrange("p (n h) -> p n h", h=H)
            return e.tensor_mul(ar("scr")[:, 0:nb_ch * H].rearrange("p (n h) -> p n h", h=H),
                                e8v, alb)
        B.emit(V, e8mul, reads=(ark("alpha"), "e8s"), writes=(ark("scr"),))
        bank, bk = pool.acquire("f32")
        B.emit(P_, lambda e, bank=bank: e.matmul(bank[:, 0:nb_ch * H], ones8[:],
                                                 ar("scr")[:, 0:nb_ch * H], start=True, stop=True),
               reads=(ark("scr"), "ones8"), writes=(bk,))
        B.emit(V, lambda e, bank=bank: e.tensor_copy(aB[:], bank[:, 0:nb_ch * H]),
               reads=(bk,), writes=("aB",))
        return

    def prescan_rows(s, b, h, rset):
        for (row, srcnm) in ((0, "cum"), (2, "lnrk"), (3, "lnbrk")):
            B.emit(SY, lambda e, row=row, srcnm=srcnm, h=h, rset=rset: e.dma_start(
                out=Lrow[rset][row:row + 1, :], in_=ar(srcnm)[h:h + 1, :]),
                reads=(ark(srcnm),), writes=(f"Lr{rset}",), dma=True)
        for (off, srcnm) in ((0, "m1"), (64, "m2"), (128, "m3")):
            def rdma(e, off=off, srcnm=srcnm, h=h, rset=rset):
                dst = Rr[rset][1:2, :].rearrange("p (n q) -> p n q", q=192)[:, :, off:off + 64]
                return e.dma_start(out=dst, in_=ar(srcnm)[h:h + 1, :].rearrange(
                    "p (n q) -> p n q", q=64))
            B.emit(SY, rdma, reads=(ark(srcnm),), writes=(f"Rr{rset}",), dma=True)

    def prescan_head(s, b, h, dset):
        rset = dset
        nbh = (nb_ch + 1) // 2
        banks = [pool.acquire("f32") for _ in range((nb_ch + 1) // 2)]
        for ci in range(nb_ch):
            sl = slice(ci * CH, (ci + 1) * CH)
            bank, bk = banks[ci // 2]
            B.emit(P_, lambda e, sl=sl, ci=ci, bank=bank: e.matmul(
                bank[0:64, (ci % 2) * 192:(ci % 2) * 192 + 192], Lrow[rset][:, sl],
                Rr[rset][:, ci * 192:(ci + 1) * 192], start=True, stop=True),
                reads=(f"Lr{rset}", f"Rr{rset}"), writes=(bk,))
        for bi, (bank, bk) in enumerate(banks):
            w2 = min(nb_ch - bi * 2, 2) * 192
            B.emit(V, lambda e, bank=bank, bi=bi, w2=w2: e.tensor_add(
                bank[0:64, 0:w2], bank[0:64, 0:w2], mall[:, bi * 384: bi * 384 + w2]),
                reads=(bk, "mall"), writes=(bk,))
            B.emit(A_, lambda e, bank=bank, bi=bi, w2=w2: e.activation(
                Dall[dset][:, bi * 384: bi * 384 + w2], bank[0:64, 0:w2], AF.Exp),
                reads=(bk,), writes=(f"Dall{dset}",))
        aset = h % NASET
        for c2 in range(nb_c2):
            prescan_grams(s, b, h, c2, dset, aset)

    def prescan_grams(s, b, h, c2, dset, aset):
        bank, bk = pool.acquire("f32")
        for par in range(2):
            ci = c2 * 2 + par
            B.emit(P_, lambda e, ci=ci, par=par, bank=bank: e.matmul(
                bank[0:64, par * 128: par * 128 + 64], ksl(h, ci), ksl(h, ci),
                start=True, stop=True), reads=(f"qkv{8 + h}",), writes=(bk,))
            B.emit(P_, lambda e, ci=ci, par=par, bank=bank: e.matmul(
                bank[0:64, par * 128 + 64: par * 128 + 128], ksl(h, ci), qsl(h, ci),
                start=True, stop=True), reads=(f"qkv{8 + h}", f"qkv{h}"), writes=(bk,))
        for par in range(2):
            ci = c2 * 2 + par
            pb = par * 64
            co = c2 * 128 + pb
            B.emit(V, lambda e, ci=ci, pb=pb, co=co, par=par, bank=bank: e.tensor_mul(
                Ab[aset][pb:pb + 64, co:co + 64], bank[0:64, par * 128: par * 128 + 64],
                Dall[dset][:, ci * 192:ci * 192 + 64]),
                reads=(bk, f"Dall{dset}"), writes=(f"Ab{aset}",))
            B.emit(V, lambda e, ci=ci, pb=pb, co=co, par=par, bank=bank: e.tensor_mul(
                ATb[aset][pb:pb + 64, co:co + 64], bank[0:64, par * 128: par * 128 + 64],
                Dall[dset][:, ci * 192 + 64:ci * 192 + 128]),
                reads=(bk, f"Dall{dset}"), writes=(f"ATb{aset}",))
        def at_mul(e):
            kq = bank[0:64, 0:256].rearrange("p (n q) -> p n q", q=128)[:, :, 64:128]
            dui = Dall[dset][:, :].rearrange("p (n q) -> p n q", q=192)[
                :, 2 * c2:2 * c2 + 2, 128:192]
            at = attnT[h][c2][:].rearrange("p (n q) -> p n q", q=64)
            return e.tensor_mul(at, kq, dui)
        B.emit(V, at_mul, reads=(bk, f"Dall{dset}"), writes=(f"aT{h}_{c2}",))

    def factor_chains(s, b, heads):
        SLOT = {"ImAT": 0, "A2": 1, "A2T": 2, "T1": 3, "A4": 0, "A4T": 4,
                "T2": 5, "A8": 1, "A8T": 2, "T3": 3, "A16": 0, "A16T": 4,
                "T4": 5, "A32": 1}
        ctx = []
        for (h, aset, fset) in heads:
            ctx.append((h, Ab[aset], f"Ab{aset}", ATb[aset], f"ATb{aset}",
                        fac[fset], fset))
        def tile(c, nm):
            (h, Abt, kA, ATt, kAT, FS, fset) = c
            if nm == "A": return Abt, kA
            if nm == "AT": return ATt, kAT
            s = SLOT[nm]
            return FS[s], f"fs{s}_{fset}"
        def sq_step(c, dstnm, ltnm, rhnm, evac_eng):
            dst, dkey = tile(c, dstnm)
            lt, lkey = tile(c, ltnm)
            rh, rkey = tile(c, rhnm)
            bank2, bk2 = pool.acquire("f32")
            for p in range(2):
                B.emit(P_, lambda e, bank2=bank2, lt=lt, rh=rh, p=p: e.matmul(
                    bank2[:, p * 128:(p + 1) * 128], lt[:, p * 128:(p + 1) * 128],
                    rh[:, p * 128:(p + 1) * 128], start=True, stop=True),
                    reads=(lkey, rkey), writes=(bk2,))
            if evac_eng == A_:
                B.emit(A_, lambda e, bank2=bank2, dst=dst: e.activation(
                    dst[:], bank2[:, 0:256], AF.Copy), reads=(bk2,), writes=(dkey,))
            else:
                B.emit(V, lambda e, bank2=bank2, dst=dst: e.tensor_copy(dst[:], bank2[:, 0:256]),
                       reads=(bk2,), writes=(dkey,))
        def chain_step(c, dstnm, pwnm, rhnm):
            dst, dkey = tile(c, dstnm)
            pw, pwkey = tile(c, pwnm)
            rh, rkey = tile(c, rhnm)
            bank2, bk2 = pool.acquire("f32")
            for p in range(2):
                B.emit(P_, lambda e, bank2=bank2, pw=pw, rh=rh, p=p: e.matmul(
                    bank2[:, p * 128:(p + 1) * 128], pw[:, p * 128:(p + 1) * 128],
                    rh[:, p * 128:(p + 1) * 128], start=(p == 0), stop=False),
                    reads=(pwkey, rkey), writes=(bk2,))
            B.emit(P_, lambda e, bank2=bank2, rh=rh: e.matmul(
                bank2[:, 0:256], identb[:], rh[:, 0:256], start=False, stop=True),
                reads=("identb", rkey), writes=(bk2,))
            B.emit(A_, lambda e, bank2=bank2, dst=dst: e.activation(
                dst[:], bank2[:, 0:256], AF.Copy), reads=(bk2,), writes=(dkey,))
        def sub_step(c):
            dst, dkey = tile(c, "ImAT")
            ATt, kAT = tile(c, "AT")
            B.emit(V, lambda e, dst=dst, ATt=ATt: e.tensor_sub(dst[:], identf2[:], ATt[:]),
                   reads=("identf2", kAT), writes=(dkey,))
        steps = [
            sub_step,
            lambda c, k=0: sq_step(c, "A2", "AT", "A", A_ if k % 2 else V),
            lambda c, k=1: sq_step(c, "A2T", "A", "AT", A_ if k % 2 else V),
            lambda c: chain_step(c, "T1", "A2", "ImAT"),
            lambda c, k=0: sq_step(c, "A4", "A2T", "A2", A_ if k % 2 else V),
            lambda c, k=1: sq_step(c, "A4T", "A2", "A2T", A_ if k % 2 else V),
            lambda c: chain_step(c, "T2", "A4", "T1"),
            lambda c, k=0: sq_step(c, "A8", "A4T", "A4", A_ if k % 2 else V),
            lambda c, k=1: sq_step(c, "A8T", "A4", "A4T", A_ if k % 2 else V),
            lambda c: chain_step(c, "T3", "A8", "T2"),
            lambda c, k=0: sq_step(c, "A16", "A8T", "A8", A_ if k % 2 else V),
            lambda c, k=1: sq_step(c, "A16T", "A8", "A8T", A_ if k % 2 else V),
            lambda c: chain_step(c, "T4", "A16", "T3"),
            lambda c, k=0: sq_step(c, "A32", "A16T", "A16", A_ if k % 2 else V),
        ]
        for stp in steps:
            for c in ctx:
                stp(c)
        for c in ctx:
            (h, Abt, kA, ATt, kAT, FS, fset) = c
            a32, k32 = tile(c, "A32")
            t4, k4 = tile(c, "T4")
            bank2, bk2 = pool.acquire("f32")
            for p in range(2):
                B.emit(P_, lambda e, bank2=bank2, a32=a32, t4=t4, p=p: e.matmul(
                    bank2[:, p * 128:(p + 1) * 128], a32[:, p * 128:(p + 1) * 128],
                    t4[:, p * 128:(p + 1) * 128], start=True, stop=True),
                    reads=(k32, k4), writes=(bk2,))
            for c2 in range(nb_c2):
                for par in range(2):
                    pb = par * 64
                    co = c2 * 128 + pb
                    B.emit(V, lambda e, bank2=bank2, t4=t4, h=h, c2=c2, par=par, pb=pb, co=co:
                           e.tensor_add(TmT[h][c2][par][:], bank2[pb:pb + 64, co:co + 64],
                                        t4[pb:pb + 64, co:co + 64]),
                           reads=(bk2, k4), writes=(f"TmT{h}_{c2}_{par}",))

    def c2_prep(s, b, c2):
        bank, bk = pool.acquire("f32")
        for gi, nm in enumerate(("sqg", "skgN", "skl", "beta", "cum")):
            B.emit(P_, lambda e, gi=gi, nm=nm, bank=bank: e.transpose(
                bank[:, gi * 8:(gi + 1) * 8], ar(nm)[:, c2 * 128:(c2 + 1) * 128],
                identf[0:8, 0:8]),
                reads=(ark(nm), "identf"), writes=(bk,))
        B.emit(V, lambda e, bank=bank: e.tensor_copy(scT[:], bank[:, 0:40]),
               reads=(bk,), writes=("scT",))
        for kind, tiles, scol in (("k", kp_t, 16), ("v", vb_t, 24)):
            bankb, bkb = pool.acquire("bf16")
            for h in range(H):
                src = ksl2(h, c2) if kind == "k" else vsl2(h, c2)
                ti = (1 if kind == "k" else 2) * 8 + h
                B.emit(P_, lambda e, h=h, src=src, bankb=bankb: e.transpose(
                    bankb[:, h * 128:(h + 1) * 128], src, identb[:]),
                    reads=(f"qkv{ti}", "identb"), writes=(bkb,))
            for par in range(2):
                def evac(e, par=par, bankb=bankb, tiles=tiles, scol=scol):
                    pv = bankb[par * 64:(par + 1) * 64, :].rearrange("p (h d) -> p h d", d=128)
                    scs = scT[par * 64:(par + 1) * 64, scol:scol + 8]
                    scb = scs.unsqueeze(2).broadcast_to([64, H, 128])
                    dst = tiles[par][:].rearrange("p (h d) -> p h d", d=128)
                    return e.tensor_mul(dst, pv, scb)
                B.emit(V, evac, reads=(bkb, "scT"), writes=(f"{'kp' if kind=='k' else 'vb'}{par}",))

    def scan_step(s, b, ci):
        par, c2 = ci % 2, ci // 2
        kP, kV = [], []
        bankP = []
        for half in range(2):
            bank, bk = pool.acquire("f32")
            bankP.append((bank, bk))
            for hh in range(4):
                h = half * 4 + hh
                B.emit(P_, lambda e, h=h, hh=hh, bank=bank: e.matmul(
                    bank[0:64, hh * 128:(hh + 1) * 128], ksl(h, ci),
                    S_t[:, h * 128:(h + 1) * 128], start=True, stop=True),
                    reads=(f"qkv{8 + h}", f"S{h // 4}"), writes=(bk,))
        for half, (bank, bk) in enumerate(bankP):
            def rmul(e, half=half, bank=bank):
                pv = bank[0:64, 0:512].rearrange("p (h d) -> p h d", d=128)
                scs = scT[par * 64:(par + 1) * 64, 8 + half * 4: 8 + half * 4 + 4]
                scb = scs.unsqueeze(2).broadcast_to([64, 4, 128])
                dst = Rs_t[:, half * 512:(half + 1) * 512].rearrange("p (h d) -> p h d", d=128)
                return e.tensor_mul(dst, pv, scb)
            B.emit(V, rmul, reads=(bk, "scT"), writes=(f"Rs{half}",))
        for half in range(2):
            B.emit(V, lambda e, half=half: e.tensor_add(
                R_t[:, half * 512:(half + 1) * 512], Rs_t[:, half * 512:(half + 1) * 512],
                vb_t[par][:, half * 512:(half + 1) * 512]),
                reads=(f"Rs{half}", f"vb{par}"), writes=(f"R{half}",))
        bankV = []
        for half in range(2):
            bank, bk = pool.acquire("f32")
            bankV.append((bank, bk))
            for hh in range(4):
                h = half * 4 + hh
                B.emit(P_, lambda e, h=h, hh=hh, bank=bank: e.matmul(
                    bank[0:64, hh * 128:(hh + 1) * 128], TmT[h][c2][par][:],
                    R_t[:, h * 128:(h + 1) * 128], start=True, stop=True),
                    reads=(f"TmT{h}_{c2}_{par}", f"R{h // 4}"), writes=(bk,))
            B.emit(A_, lambda e, half=half, bank=bank: e.activation(
                vn_t[:, half * 512:(half + 1) * 512], bank[0:64, 0:512], AF.Copy),
                reads=(bk,), writes=(f"vn{half}",))
        bankO = []
        for half in range(2):
            bank, bk = pool.acquire("f32")
            bankO.append((bank, bk))
            for hh in range(4):
                h = half * 4 + hh
                B.emit(P_, lambda e, h=h, hh=hh, bank=bank: e.matmul(
                    bank[0:64, hh * 128:(hh + 1) * 128], qsl(h, ci),
                    S_t[:, h * 128:(h + 1) * 128], start=True, stop=True),
                    reads=(f"qkv{h}", f"S{h // 4}"), writes=(bk,))
            def omul(e, half=half, bank=bank):
                pv = bank[0:64, 0:512].rearrange("p (h d) -> p h d", d=128)
                scs = scT[par * 64:(par + 1) * 64, half * 4: half * 4 + 4]
                scb = scs.unsqueeze(2).broadcast_to([64, 4, 128])
                dst = h_t[:, half * 512:(half + 1) * 512].rearrange("p (h d) -> p h d", d=128)
                return e.tensor_mul(dst, pv, scb)
            B.emit(V, omul, reads=(bk, "scT"), writes=("ht",))
        for half in range(2):
            bank, bk = pool.acquire("f32")
            for hh in range(4):
                h = half * 4 + hh
                B.emit(P_, lambda e, h=h, hh=hh, bank=bank: e.matmul(
                    bank[0:64, hh * 128:(hh + 1) * 128], attnT[h][c2][:, par * 64:(par + 1) * 64],
                    vn_t[:, h * 128:(h + 1) * 128], start=True, stop=True),
                    reads=(f"aT{h}_{c2}", f"vn{h // 4}"), writes=(bk,))
            B.emit(V, lambda e, half=half, bank=bank: e.tensor_add(
                h_t[:, half * 512:(half + 1) * 512], h_t[:, half * 512:(half + 1) * 512],
                bank[0:64, 0:512]), reads=(bk, "ht"), writes=("ht",))
        for half in range(2):
            def smul(e, half=half):
                sv = S_t[:, half * 512:(half + 1) * 512].rearrange("p (h d) -> p h d", d=128)
                ab = aB[:, ci * H + half * 4: ci * H + half * 4 + 4]
                abb = ab.unsqueeze(2).broadcast_to([128, 4, 128])
                dst = Sscr[:, half * 512:(half + 1) * 512].rearrange("p (h d) -> p h d", d=128)
                return e.tensor_mul(dst, sv, abb)
            B.emit(G_, smul, reads=(f"S{half}", "aB"), writes=(f"Sscr{half}",))
        for half in range(2):
            bank, bk = pool.acquire("f32")
            for hh in range(4):
                h = half * 4 + hh
                B.emit(P_, lambda e, h=h, hh=hh, bank=bank: e.matmul(
                    bank[:, hh * 128:(hh + 1) * 128], kp_t[par][:, h * 128:(h + 1) * 128],
                    vn_t[:, h * 128:(h + 1) * 128], start=(hh == 0), stop=False),
                    reads=(f"kp{par}", f"vn{h // 4}"), writes=(bk,))
            B.emit(P_, lambda e, half=half, bank=bank: e.matmul(
                bank[:, 0:512], identb[:], Sscr[:, half * 512:(half + 1) * 512],
                start=False, stop=True), reads=("identb", f"Sscr{half}"), writes=(bk,))
            B.emit(A_, lambda e, half=half, bank=bank: e.activation(
                S_t[:, half * 512:(half + 1) * 512], bank[:, 0:512], AF.Copy),
                reads=(bk,), writes=(f"S{half}",))
        B.emit(G_, lambda e: e.tensor_mul(Rs_t[:], h_t[:], h_t[:]), reads=("ht",),
               writes=("Rs0", "Rs1"))
        B.emit(V, lambda e: e.reduce_sum(rst_t[:], Rs_t[:].rearrange("p (h d) -> p h d", d=128),
                                         axis=AX.X), reads=("Rs0", "Rs1"), writes=("rst",))
        B.emit(A_, lambda e: e.activation(rst_t[:], rst_t[:], AF.Ln, scale=1.0 / DV,
                                          bias=eps6[0:64, 0:1]),
               reads=("rst", "eps6"), writes=("rst",))
        B.emit(A_, lambda e: e.activation(rst_t[:], rst_t[:], AF.Exp, scale=-0.5),
               reads=("rst",), writes=("rst",))

        def hgmul(e):
            hv = h_t[:].rearrange("p (h d) -> p h d", d=128)
            rb = rst_t[:].unsqueeze(2).broadcast_to([64, H, 128])
            return e.tensor_mul(Rs_t[:].rearrange("p (h d) -> p h d", d=128), hv, rb)
        B.emit(G_, hgmul, reads=("ht", "rst"), writes=("Rs0", "Rs1"))
        B.emit(V, lambda e: e.tensor_mul(hg_t[c2 % 2][par * 64:(par + 1) * 64, :], Rs_t[:],
                                         z_tm[c2][par][:]),
               reads=("Rs0", "Rs1", f"z{c2}_{par}"), writes=("hg0",))

    def c2_out(s, b, c2):
        hgk = "hg0"
        bankb, bkb = pool.acquire("bf16")
        for h in range(H):
            B.emit(P_, lambda e, h=h, bankb=bankb: e.transpose(
                bankb[:, h * 128:(h + 1) * 128], hg_t[c2 % 2][:, h * 128:(h + 1) * 128], identb[:]),
                reads=(hgk, "identb"), writes=(bkb,))
        B.emit(A_, lambda e, bankb=bankb: e.activation(hgf[:], bankb[:, 0:1024], AF.Copy),
               reads=(bkb,), writes=("hgf",))
        bankW = []
        for ob in range(2):
            bank, bk = pool.acquire("f32")
            bankW.append((bank, bk))
            for h in range(H):
                B.emit(P_, lambda e, h=h, ob=ob, bank=bank: e.matmul(
                    bank[:, 0:512], hgf[:, h * 128:(h + 1) * 128],
                    wo[h][:, ob * 512:(ob + 1) * 512], start=(h == 0), stop=(h == 7)),
                    reads=("hgf", f"wo{h}"), writes=(bk,))
        for half, (bank, bk) in enumerate(bankW):
            B.emit(V, lambda e, half=half, bank=bank: e.tensor_add(
                y1_t[:, half * 512:(half + 1) * 512], bank[:, 0:512],
                xf[c2][:, half * 512:(half + 1) * 512]),
                reads=(bk, f"xf{c2}"), writes=("y1",))
        ln_norm(y1_t, "y1", skeys=("stage", "Rs0", "Rs1"))
        B.emit(V, lambda e: e.reduce_max(stage2[:, 1:2], y1_t[:], axis=AX.X,
                                         apply_absolute_value=True),
               reads=("y1",), writes=("stage2",))
        B.emit(V, lambda e: e.tensor_max(amax_col[:, 0:1], amax_col[:, 0:1], stage2[:, 1:2]),
               reads=("stage2", "amax"), writes=("amax",))
        r0 = s * T + b * BLK + c2 * 128
        r = dma(G_, y1sp_d[r0:r0 + 128, :], y1_t[:], (), rkeys=("y1",))
        if phase == "attn":
            out_refs.append(r)
        dbg_store("y1", y1_t[:], ("y1",), row0=r0)

    def ln_norm(tile, key, scr=None, s2=None, skeys=("stage",), s2key="stage2", eng=V):
        if scr is None: scr = stage
        if s2 is None: s2 = stage2
        B.emit(A_, lambda e: e.activation(scr[:, 0:D], tile[:], AF.Copy,
                                          accum_out=s2[:, 2:3]),
               reads=(key,), writes=(*skeys, s2key))
        B.emit(A_, lambda e: e.activation(scr[:, 0:D], tile[:], AF.Square,
                                          accum_out=s2[:, 3:4]),
               reads=(key,), writes=(*skeys, s2key))
        B.emit(eng, lambda e: e.tensor_scalar(s2[:, 4:5], s2[:, 2:3], 1.0 / D, None,
                                            ALU.mult), reads=(s2key,), writes=(s2key,))
        B.emit(eng, lambda e: e.tensor_mul(s2[:, 5:6], s2[:, 4:5], s2[:, 4:5]),
               reads=(s2key,), writes=(s2key,))
        B.emit(eng, lambda e: e.tensor_scalar(s2[:, 6:7], s2[:, 3:4], 1.0 / D, None,
                                            ALU.mult), reads=(s2key,), writes=(s2key,))
        B.emit(eng, lambda e: e.tensor_sub(s2[:, 6:7], s2[:, 6:7], s2[:, 5:6]),
               reads=(s2key,), writes=(s2key,))
        B.emit(A_, lambda e: e.activation(s2[:, 7:8], s2[:, 6:7], AF.Ln,
                                          bias=eps5[:, 0:1]),
               reads=(s2key, "eps5"), writes=(s2key,))
        B.emit(A_, lambda e: e.activation(s2[:, 8:9], s2[:, 7:8], AF.Exp, scale=-0.5),
               reads=(s2key,), writes=(s2key,))
        B.emit(eng, lambda e: e.tensor_scalar(tile[:], tile[:], s2[:, 4:5], s2[:, 8:9],
                                            ALU.subtract, ALU.mult),
               reads=(key, s2key), writes=(key,))

    def ffn_phase():
        B.global_barrier()
        attn_stack.close()
        _scope[0] = st
        biasB = sb("biasB", [128, D], F32)
        fbrow = sb("fbrow", [1, D], F32)
        wq8 = [sb(f"wq8_{c}", [128, D], FP8) for c in range(8)]
        x8 = sb("x8", [128, D], FP8)
        fscr = sb("fscr", [128, 512], F32)
        NFB = 8
        fy = [sb(f"fy{i}", [128, D], F32) for i in range(NFB)]
        fsc = [sb(f"fsc{i}", [128, D], F32) for i in range(NFB)]
        fs2 = [sb(f"fs2{i}", [128, 16], F32) for i in range(NFB)]
        fx8 = [sb(f"fx8{i}", [128, D], FP8) for i in range(NFB)]
        dma(SY, fbrow[:], fb_d.rearrange("(one d) -> one d", one=1), ("fbrow",))
        for half in range(2):
            bank2, bk2 = pool.acquire("f32")
            B.emit(P_, lambda e, half=half, bank2=bank2: e.matmul(
                bank2[:, 0:512], ones1r[:], fbrow[0:1, half * 512:(half + 1) * 512],
                start=True, stop=True), reads=("ones1r", "fbrow"), writes=(bk2,))
            B.emit(V, lambda e, half=half, bank2=bank2: e.tensor_copy(
                biasB[:, half * 512:(half + 1) * 512], bank2[:, 0:512]),
                reads=(bk2,), writes=("biasB",))
        if phase != "ffn":
            bank, bk = pool.acquire("f32")
            B.emit(P_, lambda e, bank=bank: e.transpose(bank[0:1, 0:128], amax_col[:, 0:1],
                                                        identf[:]),
                   reads=("amax", "identf"), writes=(bk,))
            B.emit(V, lambda e, bank=bank: e.reduce_max(gsc[0:1, 0:1], bank[0:1, 0:128], axis=AX.X),
                   reads=(bk,), writes=("gsc",))
        B.emit(V, lambda e: e.memset(amax_col[:], 0.0), writes=("amax",))
        for c in range(8):
            bt, bkk = fy[c % NFB], f"fy{c % NFB}"
            dma(G_ if c % 2 else SY, bt[:], fw_d[c * 128:(c + 1) * 128, :], (bkk,))
            B.emit(V, lambda e, bt=bt, c=c, bkk=bkk: e.reduce_max(
                fs2[c % NFB][:, 0:1], bt[:], axis=AX.X, apply_absolute_value=True),
                   reads=(bkk,), writes=(f"fs2{c % NFB}",))
            B.emit(V, lambda e, c=c: e.tensor_max(amax_col[:, 0:1], amax_col[:, 0:1],
                                             fs2[c % NFB][:, 0:1]),
                   reads=(f"fs2{c % NFB}", "amax"), writes=("amax",))
        bank, bk = pool.acquire("f32")
        B.emit(P_, lambda e, bank=bank: e.transpose(bank[0:1, 0:128], amax_col[:, 0:1],
                                                    identf[:]),
               reads=("amax", "identf"), writes=(bk,))
        B.emit(V, lambda e, bank=bank: e.reduce_max(gsc[0:1, 2:3], bank[0:1, 0:128], axis=AX.X),
               reads=(bk,), writes=("gsc",))
        if phase == "ffn":
            dma(SY, stage2[0:1, 0:8], amax8_d[0:1, 0:8], ("stage2",))
            B.emit(V, lambda e: e.reduce_max(gsc[0:1, 1:2], stage2[0:1, 0:8], axis=AX.X),
                   reads=("stage2",), writes=("gsc",))
        elif do_coll and n_cores > 1:
            dma(SY, cci_d[0:1, 0:8], gsc[0:1, 0:8], ("cci",), rkeys=("gsc",))
            B.emit(G_, lambda e: e.collective_compute(
                "AllReduce", ALU.max, [list(range(n_cores))], [cci_d[0:1, 0:8]],
                [cco_d[0:1, 0:8]]), reads=("cci",), writes=("cco",), dma=True)
            dma(SY, gsc[0:1, 1:2], cco_d[0:1, 0:1], ("gsc",), rkeys=("cco",))
        else:
            B.emit(V, lambda e: e.tensor_copy(gsc[0:1, 1:2], gsc[0:1, 0:1]),
                   reads=("gsc",), writes=("gsc",))
        B.emit(V, lambda e: e.reciprocal(gsc[0:1, 3:4], gsc[0:1, 1:2]), reads=("gsc",), writes=("gsc",))
        B.emit(V, lambda e: e.tensor_scalar(gsc[0:1, 4:5], gsc[0:1, 3:4], 224.0, None, ALU.mult),
               reads=("gsc",), writes=("gsc",))
        B.emit(V, lambda e: e.reciprocal(gsc[0:1, 5:6], gsc[0:1, 2:3]), reads=("gsc",), writes=("gsc",))
        B.emit(V, lambda e: e.tensor_scalar(gsc[0:1, 6:7], gsc[0:1, 5:6], 224.0, None, ALU.mult),
               reads=("gsc",), writes=("gsc",))
        B.emit(V, lambda e: e.tensor_mul(gsc[0:1, 7:8], gsc[0:1, 1:2], gsc[0:1, 2:3]),
               reads=("gsc",), writes=("gsc",))
        B.emit(V, lambda e: e.tensor_scalar(gsc[0:1, 7:8], gsc[0:1, 7:8], 1.0 / (224.0 * 224.0),
                                            None, ALU.mult), reads=("gsc",), writes=("gsc",))
        for col, tile, tkey in ((4, sx_col, "sxcol"), (6, sw_col, "swcol"), (7, fs_col, "fscol")):
            bank2, bk2 = pool.acquire("f32")
            B.emit(P_, lambda e, col=col, bank2=bank2: e.matmul(
                bank2[:, 0:1], ones1r[:], gsc[0:1, col:col + 1], start=True, stop=True),
                reads=("gsc", "ones1r"), writes=(bk2,))
            B.emit(V, lambda e, tile=tile, bank2=bank2: e.tensor_copy(tile[:], bank2[:, 0:1]),
                   reads=(bk2,), writes=(tkey,))
        for c in range(8):
            bt, bkk = fy[c % NFB], f"fy{c % NFB}"
            fsr, fsk = fsc[c % NFB], f"fsc{c % NFB}"
            dma(G_ if c % 2 else SY, bt[:], fw_d[c * 128:(c + 1) * 128, :], (bkk,))
            for q in range(2):
                bank2, bk2 = pool.acquire("f32")
                for j in range(4):
                    ic = q * 4 + j
                    B.emit(P_, lambda e, ic=ic, j=j, bank2=bank2, bt=bt: e.transpose(
                        bank2[:, j * 128:(j + 1) * 128], bt[:, ic * 128:(ic + 1) * 128],
                        identf[:]), reads=(bkk, "identf"), writes=(bk2,))
                B.emit(V, lambda e, bank2=bank2, fsr=fsr, q=q: e.tensor_scalar(
                    fsr[:, q * 512:(q + 1) * 512], bank2[:, 0:512], sw_col[:, 0:1], None,
                    ALU.mult), reads=(bk2, "swcol"), writes=(fsk,))
                for j in range(4):
                    ic = q * 4 + j
                    B.emit(V, lambda e, ic=ic, j=j, c=c, fsr=fsr, q=q: e.tensor_copy(
                        wq8[ic][:, c * 128:(c + 1) * 128],
                        fsr[:, q * 512 + j * 128: q * 512 + (j + 1) * 128]),
                        reads=(fsk,), writes=(f"wq8_{ic}",))
        for s in range(n_seq):
            for tt in range(T // 128):
                ib = (s * (T // 128) + tt) % NFB
                yt, sc, s2, xq = fy[ib], fsc[ib], fs2[ib], fx8[ib]
                ky, ks, k2, kx = f"fy{ib}", f"fsc{ib}", f"fs2{ib}", f"fx8{ib}"
                r0 = s * T + tt * 128
                dma(G_, yt[:], (y1in_d if phase == "ffn" else y1sp_d)[r0:r0 + 128, :],
                    (ky,))
                B.emit(A_, lambda e, yt=yt, sc=sc: e.activation(
                    sc[:, 0:D], yt[:], AF.Copy, scale=sx_col[:, 0:1]),
                       reads=(ky, "sxcol"), writes=(ks,))
                for q in range(2):
                    bank2, bk2 = pool.acquire("f32")
                    for j in range(4):
                        ic = q * 4 + j
                        B.emit(P_, lambda e, ic=ic, j=j, bank2=bank2, sc=sc: e.transpose(
                            bank2[:, j * 128:(j + 1) * 128], sc[:, ic * 128:(ic + 1) * 128],
                            identf[:]), reads=(ks, "identf"), writes=(bk2,))
                    B.emit(V, lambda e, q=q, bank2=bank2, xq=xq: e.tensor_copy(
                        xq[:, q * 512:(q + 1) * 512], bank2[:, 0:512]),
                        reads=(bk2,), writes=(kx,))
                bankF = []
                for ob in range(2):
                    bank, bk = pool.acquire("f32")
                    bankF.append((bank, bk))
                    for c in range(8):
                        B.emit(P_, lambda e, c=c, ob=ob, bank=bank, xq=xq: e.matmul(
                            bank[:, 0:512], xq[:, c * 128:(c + 1) * 128],
                            wq8[c][:, ob * 512:(ob + 1) * 512], start=(c == 0), stop=(c == 7)),
                            reads=(kx, f"wq8_{c}"), writes=(bk,))
                for half, (bank, bk) in enumerate(bankF):
                    B.emit(V, lambda e, half=half, bank=bank, sc=sc: e.scalar_tensor_tensor(
                        sc[:, half * 512:(half + 1) * 512], bank[:, 0:512], fs_col[:, 0:1],
                        biasB[:, half * 512:(half + 1) * 512], ALU.mult, ALU.add),
                        reads=(bk, "fscol", "biasB"), writes=(ks,))
                B.emit(V, lambda e, yt=yt, sc=sc: e.tensor_add(yt[:], sc[:, 0:D], yt[:]),
                       reads=(ks, ky), writes=(ky,))
                ln_norm(yt, ky, scr=sc, s2=s2, skeys=(ks,), s2key=k2)
                r = dma(SY, y_d[r0:r0 + 128, :], yt[:], (), rkeys=(ky,))
                out_refs.append(r)

    if phase != "ffn":
        phase_w()
    else:
        dma(SY, identf[:], id_d, ("identf",))
        dma(G_, identf2[:, 0:128], id_d, ("identf2",))
        dma(G_, identf2[:, 128:256], id_d, ("identf2",))
        B.emit(V, lambda e: e.memset(ones1r[:], 1.0), writes=("ones1r",))
        B.emit(V, lambda e: e.memset(eps5[:], 1e-5), writes=("eps5",))
    gblk = 0
    for s in range(n_seq if phase != "ffn" else 0):
        seq_init(s)
        for b in range(n_blk):
            load_transpose_x(s, b)
            projections(s, b)
            if upto == "proj":
                for ti in range(24):
                    dbg_store("proj", qkv[ti][:, 3:3 + BLK], (f"qkv{ti}",), row0=ti * 128)
                dbg_store("zdbg", z_tm[0][:], ("z0",), row0=0)
                dbg_store("bab", ar("ba_b"), (ark("ba_b"),), row0=0)
                break
            conv(s, b)
            gchain(s, b)
            l2norm(s, b)
            if upto == "conv":
                for ti in range(24):
                    dbg_store("convo", qkv[ti][:, 3:3 + BLK], (f"qkv{ti}",), row0=ti * 128)
                dbg_store("cum", ar("cum"), (ark("cum"),), row0=0)
                dbg_store("rq", ar("rq"), (ark("rq"),), row0=0)
                dbg_store("beta", ar("beta"), (ark("beta"),), row0=0)
                break
            scalerows(s, b, gblk % 2)
            prescan_rows(s, b, 0, (gblk * H) % NRS)
            for h in range(H):
                if h + 1 < H:
                    prescan_rows(s, b, h + 1, (gblk * H + h + 1) % NRS)
                prescan_head(s, b, h, (gblk * H + h) % NRS)
                if h % 4 == 3:
                    factor_chains(s, b, [(hh, hh % NASET, hh % 4)
                                         for hh in range(h - 3, h + 1)])

            if upto == "prescan":
                dbg_store("TmT0", TmT[0][0][0][:], ("TmT0_0_0",))
                dbg_store("TmT0b", TmT[0][0][1][:], ("TmT0_0_1",))
                dbg_store("TmT7", TmT[7][0][0][:], ("TmT7_0_0",))
                dbg_store("attnT0", attnT[0][0][:], ("aT0_0",))
                dbg_store("attnT7", attnT[7][0][:], ("aT7_0",))
                break
            for ci in range(nb_ch):
                par, c2 = ci % 2, ci // 2
                if par == 0:
                    c2_prep(s, b, c2)
                scan_step(s, b, ci)
                if par == 1:
                    c2_out(s, b, c2)
            if upto == "scan" and b == 0:
                dbg_store("hg0", hg_t[0][:], ("hg0",))
                break
            gblk += 1
        if upto in ("proj", "conv", "prescan", "scan"):
            break
    if phase == "attn":
        bank, bk = pool.acquire("f32")
        B.emit(P_, lambda e, bank=bank: e.transpose(bank[0:1, 0:128], amax_col[:, 0:1], identf[:]),
               reads=("amax", "identf"), writes=(bk,))
        B.emit(V, lambda e, bank=bank: e.reduce_max(gsc[0:1, 0:1], bank[0:1, 0:128], axis=AX.X),
               reads=(bk,), writes=("gsc",))
        B.emit(V, lambda e: e.tensor_copy(gsc[0:1, 1:8], gsc[0:1, 0:1].broadcast_to([1, 7])),
               reads=("gsc",), writes=("gsc",))
        r = dma(SY, amax_d[0:1, 0:8], gsc[0:1, 0:8], (), rkeys=("gsc",))
        out_refs.append(r)
    if upto == "full" and phase in ("full", "ffn"):
        ffn_phase()
    B.finalize(final_waits=out_refs if out_refs else B.barrier_ref(["y1"]))
    attn_stack.close()
    st.close()
    return nc


def host_consts(nb_ch=8):
    ident = np.eye(128, dtype=np.float32)
    i, j = np.mgrid[0:64, 0:64]
    NEG = np.float32(-1e30)
    m_ls = np.where(i > j, 0.0, NEG).astype(np.float32)
    m_ui = np.where(j >= i, 0.0, NEG).astype(np.float32)
    m_us = np.where(j > i, 0.0, NEG).astype(np.float32)
    tile8 = lambda m: np.tile(m, (1, 8)).astype(np.float32)
    eh = np.zeros((128, 64), np.float32)
    for h in range(8):
        eh[:, h * 8 + h] = 0.0
    eh = np.zeros((128, 64), np.float32)
    for h in range(8):
        eh[:, h * 8 + h] = 1.0
    e8 = np.zeros((8, 8 * 8), np.float32)
    for c in range(8):
        for h in range(8):
            e8[h, c * 8 + h] = 1.0
    lrp = np.zeros((16, 512), np.float32)
    lrp[1, :] = 1.0
    rrp = np.zeros((4, 3, 8, 64), np.float32)
    rrp[0, 0] = 1.0; rrp[3, 0] = 1.0
    rrp[0, 1] = -1.0; rrp[2, 1] = 1.0
    rrp[0, 2] = -1.0; rrp[2, 2] = 1.0
    rrp = np.ascontiguousarray(rrp.transpose(0, 2, 1, 3).reshape(4, 1536))
    mall = np.zeros((64, 8, 3, 64), np.float32)
    for ci in range(8):
        mall[:, ci, 0] = m_ls; mall[:, ci, 1] = m_us; mall[:, ci, 2] = m_ui
    mall = np.ascontiguousarray(mall.reshape(64, 1536))
    return dict(ident=ident, mask_all=mall, rr_pat=rrp,
                eh_sel=eh, e8_sel=e8, lr_pat=lrp)


def permute_weights(W_qkvz, W_ba):
    cols = []
    for h in range(8):
        for part in range(3):
            cols.extend(range(h * 512 + part * 128, h * 512 + (part + 1) * 128))
    for h in range(8):
        cols.extend(range(h * 512 + 384, h * 512 + 512))
    Wq = np.ascontiguousarray(W_qkvz[:, cols])
    bcols = [2 * h for h in range(8)] + [2 * h + 1 for h in range(8)]
    Wb = np.ascontiguousarray(W_ba[:, bcols])
    return Wq, Wb


from concourse.bass_utils import run_bass_kernel_spmd

_CFG = dict(n_seq=2, T=2048, BLK=256, upto="full", collective=False, n_cores=8,
            two_launch=True)

def kernel(L0_attn_x, L0_attn_W_qkvz, L0_attn_W_ba, L0_attn_conv_w, L0_attn_dt_bias,
           L0_attn_A_log, L0_attn_norm_weight, L0_attn_W_out, L0_ffn_weight, L0_ffn_bias):
    n_cores = 8
    cfg = dict(_CFG)
    x = np.ascontiguousarray(np.asarray(L0_attn_x, np.float32))
    B_, T_, D_ = x.shape
    assert (B_, T_, D_) == (16, 2048, 1024)
    Wq, Wb = permute_weights(np.asarray(L0_attn_W_qkvz, np.float32),
                             np.asarray(L0_attn_W_ba, np.float32))
    consts = host_consts()
    shared = dict(wqkvz=Wq, wba=Wb,
                  convw=np.ascontiguousarray(np.asarray(L0_attn_conv_w, np.float32).reshape(3072, 4)),
                  dtb=np.ascontiguousarray(np.asarray(L0_attn_dt_bias, np.float32)),
                  alog=np.ascontiguousarray(np.asarray(L0_attn_A_log, np.float32)),
                  nw=np.ascontiguousarray(np.asarray(L0_attn_norm_weight, np.float32)),
                  wout=np.ascontiguousarray(np.asarray(L0_attn_W_out, np.float32)),
                  ffnw=np.ascontiguousarray(np.asarray(L0_ffn_weight, np.float32)),
                  ffnb=np.ascontiguousarray(np.asarray(L0_ffn_bias, np.float32)))
    shared.update(consts)
    if cfg.get("two_launch"):
        cfg1 = dict(cfg); cfg1["phase"] = "attn"
        nc1 = build(cfg1)
        in_maps = []
        for c in range(n_cores):
            m = dict(shared)
            m["x"] = np.ascontiguousarray(x[2 * c: 2 * c + 2].reshape(2 * 2048, 1024))
            in_maps.append(m)
        res1 = run_bass_kernel_spmd(nc1, in_maps, core_ids=list(range(n_cores)))
        amax8 = np.ascontiguousarray(
            np.concatenate([np.asarray(res1.results[c]["amaxout"])[0:1, 0:1]
                            for c in range(n_cores)], axis=1).astype(np.float32))
        cfg2 = dict(cfg); cfg2["phase"] = "ffn"
        nc2 = build(cfg2)
        in_maps2 = []
        for c in range(n_cores):
            m = dict(shared)
            m["x"] = np.zeros((2 * 2048, 1024), np.float32)
            m["y1in"] = np.ascontiguousarray(np.asarray(res1.results[c]["y1out"], np.float32))
            m["amax8"] = amax8
            in_maps2.append(m)
        res = run_bass_kernel_spmd(nc2, in_maps2, core_ids=list(range(n_cores)))
    else:
        nc = build(cfg)
        in_maps = []
        for c in range(n_cores):
            m = dict(shared)
            m["x"] = np.ascontiguousarray(x[2 * c: 2 * c + 2].reshape(2 * 2048, 1024))
            in_maps.append(m)
        res = run_bass_kernel_spmd(nc, in_maps, core_ids=list(range(n_cores)))
    out = np.concatenate([res.results[c]["y"].reshape(2, 2048, 1024) for c in range(n_cores)],
                         axis=0)
    return np.ascontiguousarray(out.astype(np.float32))
